# version 13
# baseline (speedup 1.0000x reference)
import math
import numpy as np
import ml_dtypes
import concourse.bass as bass
import concourse.mybir as mybir
from concourse.bass_utils import run_bass_kernel_spmd

F32 = mybir.dt.float32
BF16 = mybir.dt.bfloat16
AF = mybir.ActivationFunctionType
ALU = mybir.AluOpType
AX = mybir.AxisListType

B, SEQ, D = 2, 8192, 2048
DKV, NH, DH = 512, 16, 128
NE, EW = 16, 512
NTOK = 2048
NT_OWN = 16
NT_SEQ = 64
NT_ALL = NT_SEQ + NT_OWN
SCALE = 1.0 / math.sqrt(DH)
EPS = 1e-5
NEGB = -1e30
CAP = 1024
BIS_IT = 38

KB = 1024
SBB = 16512
OFF_TRIL = SBB + 0
OFF_BIAS = SBB + 512
OFF_EH = SBB + 640
OFF_RW = SBB + 4736
OFF_WGT = SBB + 5760
OFF_LOG = SBB + 6784
OFF_SMALL = SBB + 7808
OFF_ONES = SBB + 8448
OFF_IDF = SBB + 8704
OFF_IDB = SBB + 9216
SLAB1 = SBB + 12288
SLAB2 = SLAB1 + 64 * KB
SLAB3 = SLAB2 + 64 * KB
SLAB_END = SLAB3 + 64 * KB
assert SLAB_END <= 229344


class MK:
    def __init__(self, debug=False, coll=True):
        self.debug = debug
        self.coll = coll
        nc = bass.Bass("TRN2", target_bir_lowering=False)
        self.nc = nc
        dbg = "ExternalOutput" if debug else "Internal"

        di = lambda n, s: nc.dram_tensor(n, s, F32, kind="ExternalInput").ap()
        self.x_seq = di("x_seq", [SEQ, D])
        self.x_own = di("x_own", [NTOK, D])
        self.w_dkvT = di("w_dkvT", [D, DKV])
        self.w_dqT = di("w_dqT", [D, D])
        self.w_uqT = di("w_uqT", [D, D])
        self.w_ukT = di("w_ukT", [DKV, D])
        self.w_uvT = di("w_uvT", [DKV, D])
        self.w_oT = di("w_oT", [D, D])
        self.b_o = di("b_o", [1, D])
        self.gate_wT = di("gate_wT", [D, NE])
        self.uT = di("uT", [D, NE * EW])
        self.gT = di("gT", [D, NE * EW])
        self.dT = di("dT", [NE * EW, D])
        self.tril_in = di("tril_in", [128, 128])
        self.bias_in = di("bias_in", [128, 32])
        self.ident_in = di("ident_in", [128, 128])
        self.out = nc.dram_tensor("out", [NTOK, D], F32, kind="ExternalOutput").ap()

        dr = lambda n, s, dt, kind="Internal": nc.dram_tensor(n, s, dt, kind=kind).ap()
        self.ckv_spill = dr("ckv_spill", [16, 4, 128, 512], BF16, dbg)
        self.ckv_own = dr("ckv_own", [2, 2, 4, 128, 512], BF16, dbg)
        self.xn_spill = dr("xn_spill", [16, 128, NTOK], BF16, dbg)
        self.qT_spill = dr("qT_spill", [16, 128, NTOK], BF16, dbg)
        self.ctx_spill = dr("ctx_spill", [2, 16, 128, 1024], BF16, dbg)
        self.ir_spill = dr("ir_spill", [16, 4, 128, NTOK], BF16, dbg)
        self.rs_dram = dr("rs_dram", [2, NH, 1024], F32, dbg)
        self.cnt_dram = dr("cnt_dram", [128, 8], F32)
        self.lg_local = dr("lg_local", [NTOK, NE], F32)
        self.lg_all = dr("lg_all", [8 * NTOK, NE], F32)
        if debug:
            self.bis_dbg = dr("bis_dbg", [128, 96], F32, "ExternalOutput")
            self.thr_dbg = dr("thr_dbg", [128, 8], F32, "ExternalOutput")
            self.wgt_dbg = dr("wgt_dbg", [128, 256], F32, "ExternalOutput")
            self.log_dbg = dr("log_dbg", [128, 256], F32, "ExternalOutput")
            self.rw_dbg = dr("rw_dbg", [128, 256], F32, "ExternalOutput")

        sb = lambda n, off, shape, dt: nc.alloc_sbuf_tensor_at(n, shape, dt, offset=off).ap()
        self.tril = sb("s_tril", OFF_TRIL, [128, 128], F32)
        self.biass = sb("s_bias", OFF_BIAS, [128, 32], F32)
        self.eh = sb("s_eh", OFF_EH, [128, NH, 128], BF16)
        self.rw = sb("s_rw", OFF_RW, [128, 16, 16], F32)
        self.wgt = sb("s_wgt", OFF_WGT, [128, 16, 16], F32)
        self.log = sb("s_log", OFF_LOG, [128, 16, 16], F32)
        self.small = sb("s_small", OFF_SMALL, [128, 160], F32)
        self.ones_row = sb("s_ones", OFF_ONES, [1, 128], BF16)
        self.idf = sb("s_idf", OFF_IDF, [128, 128], F32)
        self.idb = sb("s_idb", OFF_IDB, [128, 128], BF16)

        self.pp0 = nc.alloc_psum_tensor("pp0", [128, 2048], F32).ap()
        self.pp1 = nc.alloc_psum_tensor("pp1", [128, 2048], F32).ap()

        self.sems = {}
        for name in ["sx", "sx1", "sx2", "sx3",
                     "sw", "sw1", "sk", "sk1", "sq", "sq1", "so",
                     "sy", "sy1",
                     "va", "vb", "vc", "vd", "ve", "vf",
                     "aa", "ab", "ac_", "ad",
                     "ta", "tb", "tc", "td", "te",
                     "cc"]:
            self.sems[name] = nc.alloc_semaphore(name)
        self.cnt = {k: 0 for k in self.sems}

    def sem(self, n):
        return self.sems[n]

    def bump(self, n, k=1):
        self.cnt[n] += k
        return self.cnt[n]

    def sm(self, slot, par=0):
        return self.small[:, slot + par: slot + par + 1]

    def phase_prepare(self, block):
        s = self

        def f_sync(sy):
            sy.dma_start(out=s.tril[:], in_=s.tril_in[:]).then_inc(s.sem("sx"), 16)
            sy.dma_start(out=s.biass[:], in_=s.bias_in[:]).then_inc(s.sem("sx"), 16)
            sy.dma_start(out=s.idf[:], in_=s.ident_in[:]).then_inc(s.sem("sx"), 16)
            s.bump("sx", 48)
        block.sync(f_sync)

        def f_gp(g):
            g.dma_start(out=s.idb[:], in_=s.ident_in[:]).then_inc(s.sem("sw"), 16)
            s.bump("sw", 16)
        block.gpsimd(f_gp)

        def f_vec(v):
            v.memset(s.eh[:], 0.0)
            v.drain()
            for h in range(NH):
                v.memset(s.eh[:, h, h: h + 1], 1.0)
            v.memset(s.ones_row[:], 1.0)
            v.drain().then_inc(s.sem("va"), 1)
            s.bump("va")
        block.vector(f_vec)

    def phase_A(self, block):
        s = self
        nc = s.nc
        NT = NT_ALL

        sbat = lambda n, off, shape, dt: nc.alloc_sbuf_tensor_at(n, shape, dt, offset=off).ap()
        A2 = SLAB2
        wdkv = sbat("a_wdkv", A2, [128, 16, 512], BF16)
        gwt = sbat("a_gwt", A2 + 16 * KB, [128, 16, 16], F32)
        xtile = sbat("a_x", A2 + 17 * KB, [128, 4, D], F32)
        xnf32 = sbat("a_xnf32", A2 + 49 * KB, [128, 2, D], F32)
        A3 = SLAB3
        xnTg = sbat("a_xnTg", A3, [128, 16, 512], BF16)
        xnTf = sbat("a_xnTf", A3 + 16 * KB, [128, 2, 16, 128], F32)
        ckvo = sbat("a_ckvo", A3 + 32 * KB, [128, 4, 512], BF16)
        msk1 = sbat("a_msk1", A3 + 36 * KB, [128, 16, 16], F32)
        msk2 = sbat("a_msk2", A3 + 37 * KB, [128, 16, 16], F32)
        sqs = sbat("a_sq", A3 + 38 * KB, [128, D], F32)
        xnbf = sbat("a_xnbf", A3 + 46 * KB, [128, 4, D], BF16)
        xnT_own = sbat("xnT_own", SLAB1, [128, 16, NTOK], BF16)
        self.A_bufs = dict(xnT_own=xnT_own)

        def x_src(i):
            if i < NT_SEQ:
                return s.x_seq[i * 128:(i + 1) * 128, :]
            k = i - NT_SEQ
            return s.x_own[k * 128:(k + 1) * 128, :]

        base = dict(s.cnt)

        SX = ["sx", "sx1", "sx2", "sx3"]
        def f_sync(sy):
            for i in range(NT):
                if i >= 4:
                    sy.wait_ge(s.sem("ac_"), base["ac_"] + i - 3)
                sy.dma_start(out=xtile[:, i % 4], in_=x_src(i)
                             ).then_inc(s.sem(SX[i % 4]), 16)
                s.bump(SX[i % 4], 16)
        block.sync(f_sync)

        def f_gp(g):
            g.dma_start(out=wdkv[:],
                        in_=s.w_dkvT.rearrange("(cc p) kv -> p cc kv", p=128)
                        ).then_inc(s.sem("sw"), 16)
            g.dma_start(out=gwt[:],
                        in_=s.gate_wT.rearrange("(cc p) e -> p cc e", p=128)
                        ).then_inc(s.sem("sw"), 16)
            s.bump("sw", 32)
            for gi in range(20):
                g.wait_ge(s.sem("ad"), base["ad"] + 4 * (gi + 1))
                if gi < 16:
                    dst = s.ckv_spill[gi].rearrange("kc p t -> p kc t")
                else:
                    k = gi - 16
                    dst = s.ckv_own[k // 2, k % 2].rearrange("kc p t -> p kc t")
                g.dma_start(out=dst, in_=ckvo[:]).then_inc(s.sem("sk"), 16)
                s.bump("sk", 16)
            for k in range(NT_OWN):
                g.wait_ge(s.sem("ve"), base["ve"] + k + 1)
                g.dma_start(out=s.lg_local[k * 128:(k + 1) * 128, :],
                            in_=s.log[:, k, :]).then_inc(s.sem("so"), 16)
                s.bump("so", 16)
            g.wait_ge(s.sem("so"), s.cnt["so"])
            if s.coll:
                g.collective_compute("AllGather", ALU.bypass,
                                     replica_groups=[list(range(8))],
                                     ins=[s.lg_local.opt()],
                                     outs=[s.lg_all.opt()]).then_inc(s.sem("cc"))
                s.bump("cc", 1)
            else:
                g.dma_start(out=s.lg_all[0:NTOK, :], in_=s.lg_local[:, :]
                            ).then_inc(s.sem("cc"), 16)
                s.bump("cc", 16)
            g.wait_ge(s.sem("va"), base["va"] + NT)
            g.dma_start(out=s.xn_spill.rearrange("cc p t -> p cc t"),
                        in_=xnT_own[:]).then_inc(s.sem("so"), 16)
            s.bump("so", 16)
        block.gpsimd(f_gp)

        def f_vec(v):
            def V1(i):
                q4 = i % 4
                v.wait_ge(s.sem(SX[q4]), base[SX[q4]] + 16 * (i // 4 + 1))
                v.reduce_sum(s.sm(0, 16 * q4), xtile[:, q4], axis=AX.X)
                v.drain()
                v.tensor_scalar_mul(s.sm(0, 16 * q4), s.sm(0, 16 * q4), -1.0 / D)
                v.drain().then_inc(s.sem("vb"), 1)
                s.bump("vb")

            def V2(i):
                q4 = i % 4
                v.wait_ge(s.sem("aa"), base["aa"] + i + 1)
                v.tensor_scalar(s.sm(4, 16 * q4), s.sm(2, 16 * q4), 1.0 / D, EPS,
                                ALU.mult, ALU.add)
                v.drain().then_inc(s.sem("vc"), 1)
                s.bump("vc")

            def V3(i):
                q4 = i % 4
                v.wait_ge(s.sem("ab"), base["ab"] + i + 1)
                v.reciprocal(s.sm(8, 16 * q4), s.sm(6, 16 * q4))
                v.drain()
                v.tensor_tensor(s.sm(12, 16 * q4), s.sm(8, 16 * q4),
                                s.sm(8, 16 * q4), ALU.mult)
                v.drain()
                v.tensor_tensor(s.sm(12, 16 * q4), s.sm(12, 16 * q4),
                                s.sm(4, 16 * q4), ALU.mult)
                v.drain()
                v.tensor_scalar(s.sm(12, 16 * q4), s.sm(12, 16 * q4), -0.5, 1.5,
                                ALU.mult, ALU.add)
                v.drain()
                v.tensor_tensor(s.sm(8, 16 * q4), s.sm(8, 16 * q4),
                                s.sm(12, 16 * q4), ALU.mult)
                v.drain()
                v.tensor_tensor(s.sm(10, 16 * q4), s.sm(0, 16 * q4),
                                s.sm(8, 16 * q4), ALU.mult)
                v.drain().then_inc(s.sem("vd"), 1)
                s.bump("vd")

            def V4(i):
                own = i >= NT_SEQ
                k = i - NT_SEQ
                gi = i // 4
                v.wait_ge(s.sem("ta"), base["ta"] + i + 1)
                if not own:
                    if gi >= 1:
                        v.wait_ge(s.sem("tb"), base["tb"] + gi)
                    srcb = s.pp0.bitcast(BF16)
                    half = (i % 2) * 2048
                    v.tensor_copy(
                        xnTg[:, :, (i % 4) * 128:(i % 4) * 128 + 128],
                        srcb[:, half:half + 2048]
                            .rearrange("p (cc t) -> p cc t", t=128))
                    v.drain().then_inc(s.sem("va"), 1)
                    s.bump("va")
                else:
                    if k >= 2:
                        v.wait_ge(s.sem("tc"), base["tc"] + k - 1)
                    v.tensor_copy(xnTf[:, k % 2], s.pp0[:, :]
                                  .rearrange("p (cc t) -> p cc t", t=128))
                    v.drain()
                    v.tensor_copy(xnT_own[:, :, k * 128:(k + 1) * 128],
                                  xnTf[:, k % 2])
                    v.drain().then_inc(s.sem("va"), 1)
                    s.bump("va")
                    v.wait_ge(s.sem("tc"), base["tc"] + k + 1)
                    v.tensor_copy(s.log[:, k, :], s.pp1[:, 0:16])
                    v.drain().then_inc(s.sem("ve"), 1)
                    s.bump("ve")

            for i in range(NT + 3):
                if i < NT:
                    V1(i)
                if 1 <= i < NT + 1:
                    V2(i - 1)
                if 2 <= i < NT + 2:
                    V3(i - 2)
                if 3 <= i < NT + 3:
                    V4(i - 3)
            L = s.log[:]
            m1 = s.small[:, 16:32]
            m2 = s.small[:, 32:48]
            dd = s.small[:, 48:64]
            w1 = s.small[:, 64:80]
            w2 = s.small[:, 80:96]
            v.reduce_max(m1, L, axis=AX.X)
            v.drain()
            v.tensor_tensor(msk1[:], L, m1[:, :, None].broadcast_to((128, 16, 16)),
                            ALU.is_equal)
            v.drain()
            v.scalar_tensor_tensor(msk2[:], msk1[:], -1e38, L, ALU.mult, ALU.add)
            v.drain()
            v.reduce_max(m2, msk2[:], axis=AX.X)
            v.drain()
            v.tensor_tensor(msk2[:], msk2[:],
                            m2[:, :, None].broadcast_to((128, 16, 16)), ALU.is_equal)
            v.drain()
            v.tensor_tensor(dd, m1, m2, ALU.subtract)
            v.drain().then_inc(s.sem("vb"), 1)
            s.bump("vb")
            v.wait_ge(s.sem("ab"), base["ab"] + NT + 1)
            v.tensor_scalar(w2, w1, -1.0, 1.0, ALU.mult, ALU.add)
            v.drain()
            v.tensor_tensor(s.rw[:], msk1[:],
                            w1[:, :, None].broadcast_to((128, 16, 16)), ALU.mult)
            v.drain()
            v.tensor_tensor(msk1[:], msk2[:],
                            w2[:, :, None].broadcast_to((128, 16, 16)), ALU.mult)
            v.drain()
            v.tensor_tensor(s.rw[:], s.rw[:], msk1[:], ALU.add)
            v.drain().then_inc(s.sem("ve"), 1)
            s.bump("ve")
        block.vector(f_vec)

        def f_act(sc):
            def A1(i):
                q4 = i % 4
                sc.wait_ge(s.sem("vb"), base["vb"] + i + 1)
                sc.activation(out=sqs[:], in_=xtile[:, q4], func=AF.Square,
                              bias=s.sm(0, 16 * q4), scale=1.0,
                              accum_out=s.sm(2, 16 * q4))
                sc.drain().then_inc(s.sem("aa"), 1)
                s.bump("aa")

            def A2(i):
                q4 = i % 4
                sc.wait_ge(s.sem("vc"), base["vc"] + i + 1)
                sc.sqrt(s.sm(6, 16 * q4), s.sm(4, 16 * q4))
                sc.drain().then_inc(s.sem("ab"), 1)
                s.bump("ab")

            def A3(i):
                own = i >= NT_SEQ
                q4 = i % 4
                par = i % 2
                sc.wait_ge(s.sem("vd"), base["vd"] + i + 1)
                if not own and i >= 4:
                    sc.wait_ge(s.sem("ta"), base["ta"] + i - 3)
                if own and i >= 2:
                    sc.wait_ge(s.sem("ta"), base["ta"] + i - 1)
                dst = xnbf[:, q4] if not own else xnf32[:, par]
                sc.activation(out=dst, in_=xtile[:, q4], func=AF.Identity,
                              bias=s.sm(10, 16 * q4), scale=s.sm(8, 16 * q4))
                sc.drain().then_inc(s.sem("ac_"), 1)
                s.bump("ac_")

            def ACKV(i):
                gi = i // 4
                if gi >= 1:
                    sc.wait_ge(s.sem("sk"), base["sk"] + 16 * gi)
                for kc in range(4):
                    sc.wait_ge(s.sem("td"), base["td"] + 4 * gi + kc + 1)
                    sc.copy(ckvo[:, kc, :], s.pp1[:, kc * 512:(kc + 1) * 512])
                    sc.drain().then_inc(s.sem("ad"), 1)
                    s.bump("ad")

            for i in range(NT + 2):
                if i < NT:
                    A1(i)
                if 1 <= i < NT + 1:
                    A2(i - 1)
                if 2 <= i < NT + 2:
                    A3(i - 2)
                    if (i - 2) % 4 == 3:
                        ACKV(i - 2)
            sc.wait_ge(s.sem("vb"), base["vb"] + NT + 1)
            sc.activation(out=s.small[:, 64:80], in_=s.small[:, 48:64],
                          func=AF.Sigmoid)
            sc.drain().then_inc(s.sem("ab"), 1)
            s.bump("ab")
        block.scalar(f_act)

        def f_tens(t):
            t.wait_ge(s.sem("sw"), base["sw"] + 32)
            for i in range(NT):
                own = i >= NT_SEQ
                k = i - NT_SEQ
                par = i % 2
                gi = i // 4
                t.wait_ge(s.sem("ac_"), base["ac_"] + i + 1)
                if not own:
                    if i >= 2:
                        t.wait_ge(s.sem("va"), base["va"] + i - 1)
                    dstb = s.pp0.bitcast(BF16)
                    half = par * 2048
                    for cc in range(16):
                        ins = t.transpose(
                            dstb[:, half + cc * 128: half + (cc + 1) * 128],
                            xnbf[:, i % 4][:, cc * 128:(cc + 1) * 128], s.idb[:])
                    ins.then_inc(s.sem("ta"), 1)
                    s.bump("ta")
                else:
                    t.wait_ge(s.sem("va"), base["va"] + i)
                    for cc in range(16):
                        ins = t.transpose(s.pp0[:, cc * 128:(cc + 1) * 128],
                                          xnf32[:, par][:, cc * 128:(cc + 1) * 128],
                                          s.idf[:])
                    ins.then_inc(s.sem("ta"), 1)
                    s.bump("ta")
                    t.wait_ge(s.sem("va"), base["va"] + i + 1)
                    if k % 4 == 0:
                        t.wait_ge(s.sem("ad"), base["ad"] + 4 * gi)
                    for cc in range(16):
                        ins = t.matmul(s.pp1[:, 0:16], xnTf[:, k % 2][:, cc, :],
                                       gwt[:, cc, :], start=(cc == 0), stop=(cc == 15))
                    ins.then_inc(s.sem("tc"), 1)
                    s.bump("tc")
                if i % 4 == 3:
                    t.wait_ge(s.sem("va"), base["va"] + i + 1)
                    if own:
                        t.wait_ge(s.sem("ve"), base["ve"] + k + 1)
                    if gi >= 1:
                        t.wait_ge(s.sem("ad"), base["ad"] + 4 * gi)
                    for kc in range(4):
                        for cc in range(16):
                            if own:
                                k0 = k - 3
                                rhs = xnT_own[:, cc, k0 * 128: k0 * 128 + 512]
                            else:
                                rhs = xnTg[:, cc, :]
                            ins = t.matmul(s.pp1[:, kc * 512:(kc + 1) * 512],
                                           wdkv[:, cc, kc * 128:(kc + 1) * 128],
                                           rhs, start=(cc == 0), stop=(cc == 15))
                        ins.then_inc(s.sem("td"), 1)
                        s.bump("td")
                    t.drain().then_inc(s.sem("tb"), 1)
                    s.bump("tb")
        block.tensor(f_tens)

    def phase_B(self, block):
        s = self
        nc = s.nc
        sbat = lambda n, off, shape, dt: nc.alloc_sbuf_tensor_at(n, shape, dt, offset=off).ap()
        wB = sbat("b_w", SLAB2, [128, 16, 2048], BF16)
        cqT = sbat("b_cq", SLAB3, [128, 16, 2048], BF16)
        qev = sbat("b_qev", SLAB1, [128, 2, 512], BF16)
        xnT_own = self.A_bufs["xnT_own"]
        base = dict(s.cnt)
        PS = [s.pp0[:, 0:512], s.pp0[:, 512:1024],
              s.pp0[:, 1024:1536], s.pp0[:, 1536:2048]]

        def f_gp(g):
            g.dma_start(out=wB[:],
                        in_=s.w_dqT.rearrange("(cc p) q -> p cc q", p=128)
                        ).then_inc(s.sem("sw"), 16)
            s.bump("sw", 16)
            g.wait_ge(s.sem("ta"), base["ta"] + 64)
            g.dma_start(out=wB[:],
                        in_=s.w_uqT.rearrange("(cc p) q -> p cc q", p=128)
                        ).then_inc(s.sem("sw"), 16)
            s.bump("sw", 16)
        block.gpsimd(f_gp)

        def f_tens(t):
            t.wait_ge(s.sem("sw"), base["sw"] + 16)
            t.wait_ge(s.sem("va"), base["va"])
            for n in range(64):
                g_, qc = divmod(n, 16)
                if n >= 4:
                    t.wait_ge(s.sem("va"), base["va"] + n - 3)
                for cc in range(16):
                    ins = t.matmul(PS[n % 4], wB[:, cc, qc * 128:(qc + 1) * 128],
                                   xnT_own[:, cc, g_ * 512:(g_ + 1) * 512],
                                   start=(cc == 0), stop=(cc == 15))
                ins.then_inc(s.sem("ta"), 1)
                s.bump("ta")
            t.wait_ge(s.sem("sw"), base["sw"] + 32)
            for n in range(64):
                g_, qc = divmod(n, 16)
                if n < 4:
                    t.wait_ge(s.sem("va"), base["va"] + 64)
                else:
                    t.wait_ge(s.sem("vb"), base["vb"] + n - 3)
                for cc in range(16):
                    ins = t.matmul(PS[n % 4], wB[:, cc, qc * 128:(qc + 1) * 128],
                                   cqT[:, cc, g_ * 512:(g_ + 1) * 512],
                                   start=(cc == 0), stop=(cc == 15))
                ins.then_inc(s.sem("tc"), 1)
                s.bump("tc")
        block.tensor(f_tens)

        def f_vec(v):
            v.wait_ge(s.sem("sk"), base["sk"])
            v.wait_ge(s.sem("so"), base["so"])
            for n in range(64):
                g_, qc = divmod(n, 16)
                v.wait_ge(s.sem("ta"), base["ta"] + n + 1)
                v.tensor_copy(cqT[:, qc, g_ * 512:(g_ + 1) * 512], PS[n % 4])
                v.drain().then_inc(s.sem("va"), 1)
                s.bump("va")
            for n in range(64):
                par = n % 2
                v.wait_ge(s.sem("tc"), base["tc"] + n + 1)
                if n >= 2:
                    v.wait_ge(s.sem("sq" if par == 0 else "sq1"),
                              base["sq" if par == 0 else "sq1"] + 16 * (n // 2))
                v.tensor_copy(qev[:, par], PS[n % 4])
                v.drain().then_inc(s.sem("vb"), 1)
                s.bump("vb")
        block.vector(f_vec)

        def f_sync(sy):
            for n in range(64):
                g_, qc = divmod(n, 16)
                par = n % 2
                sy.wait_ge(s.sem("vb"), base["vb"] + n + 1)
                sy.dma_start(out=s.qT_spill[qc, :, g_ * 512:(g_ + 1) * 512],
                             in_=qev[:, par]
                             ).then_inc(s.sem("sq" if par == 0 else "sq1"), 16)
                s.bump("sq" if par == 0 else "sq1", 16)
        block.sync(f_sync)

    def phase_thresh(self, block):
        s = self
        nc = s.nc
        sbat = lambda n, off, shape, dt: nc.alloc_sbuf_tensor_at(n, shape, dt, offset=off).ap()
        lg = sbat("t_lg", SLAB2, [128, 128, 16], F32)
        mske = sbat("t_mske", SLAB2 + 8 * KB, [128, 128, 16], F32)
        mskd = sbat("t_mskd", SLAB2 + 16 * KB, [128, 128, 16], F32)
        msk2 = sbat("t_msk2", SLAB2 + 24 * KB, [128, 128, 16], F32)
        rw8 = sbat("t_rw8", SLAB2 + 32 * KB, [128, 128, 8], F32)
        cmp = sbat("t_cmp", SLAB2 + 36 * KB, [128, 128, 8], F32)
        cntR = sbat("t_cntR", SLAB2 + 40 * KB, [128, 128, 8], F32)
        cnt2 = sbat("t_cnt2", SLAB2 + 44 * KB, [128, 8], F32)
        mm1 = sbat("t_m1", SLAB2 + 45 * KB, [128, 128], F32)
        mm2 = sbat("t_m2", SLAB2 + 46 * KB, [128, 128], F32)
        ww1 = sbat("t_w1", SLAB2 + 47 * KB, [128, 128], F32)
        ww2 = sbat("t_w2", SLAB2 + 48 * KB, [128, 128], F32)
        st8 = sbat("t_st8", SLAB2 + 49 * KB, [128, 8, 8], F32)
        ones128 = sbat("t_ones", SLAB2 + 50 * KB, [128, 128], F32)
        snap = sbat("t_snap", SLAB2 + 51 * KB, [128, 96], F32)
        base = dict(s.cnt)

        def f_gp(g):
            g.wait_ge(s.sem("cc"), base["cc"])
            g.wait_ge(s.sem("tc"), base["tc"])
            g.dma_start(out=lg[:],
                        in_=s.lg_all.rearrange("(p t) e -> p t e", p=128)
                        ).then_inc(s.sem("so"), 16)
            s.bump("so", 16)
        block.gpsimd(f_gp)


        def f_act(sc):
            sc.wait_ge(s.sem("vb"), base["vb"] + 1)
            sc.activation(out=ww1[:], in_=mm1[:], func=AF.Sigmoid)
            sc.drain().then_inc(s.sem("ab"), 1)
            s.bump("ab")
        block.scalar(f_act)

        def f_tens(t):
            for it in range(BIS_IT):
                t.wait_ge(s.sem("vf"), base["vf"] + it + 1)
                ins = t.matmul(s.pp0[:, 0:8], ones128[:], cnt2[:],
                               start=True, stop=True)
                ins.then_inc(s.sem("tb"), 1)
                s.bump("tb")
        block.tensor(f_tens)

        def f_vec(v):
            v.wait_ge(s.sem("so"), base["so"] + 16)
            v.reduce_max(mm1[:], lg[:], axis=AX.X)
            v.drain()
            v.tensor_tensor(mske[:], lg[:],
                            mm1[:, :, None].broadcast_to((128, 128, 16)), ALU.is_equal)
            v.drain()
            v.scalar_tensor_tensor(mskd[:], mske[:], -1e38, lg[:], ALU.mult, ALU.add)
            v.drain()
            v.reduce_max(mm2[:], mskd[:], axis=AX.X)
            v.drain()
            v.tensor_tensor(msk2[:], mskd[:],
                            mm2[:, :, None].broadcast_to((128, 128, 16)), ALU.is_equal)
            v.drain()
            v.tensor_tensor(mm1[:], mm1[:], mm2[:], ALU.subtract)
            v.drain().then_inc(s.sem("vb"), 1)
            s.bump("vb")
            v.wait_ge(s.sem("ab"), base["ab"] + 1)
            v.tensor_scalar(ww2[:], ww1[:], -1.0, 1.0, ALU.mult, ALU.add)
            v.drain()
            v.tensor_tensor(rw8[:], mske[:, :, 8:16],
                            ww1[:, :, None].broadcast_to((128, 128, 8)), ALU.mult)
            v.drain()
            v.tensor_tensor(cmp[:], msk2[:, :, 8:16],
                            ww2[:, :, None].broadcast_to((128, 128, 8)), ALU.mult)
            v.drain()
            v.tensor_tensor(rw8[:], rw8[:], cmp[:], ALU.add)
            v.drain()
            v.memset(st8[:, 0, :], 0.0)
            v.memset(st8[:, 2, :], 1.0)
            v.memset(ones128[:], 1.0)
            v.drain()
            for it in range(BIS_IT):
                par = it % 2
                lo, hi = st8[:, 0 + par, :], st8[:, 2 + par, :]
                lo_n, hi_n = st8[:, 1 - par, :], st8[:, 3 - par, :]
                mid, ge, c8 = st8[:, 4, :], st8[:, 5, :], st8[:, 6, :]
                v.tensor_tensor(mid, lo, hi, ALU.add)
                v.drain()
                v.tensor_scalar_mul(mid, mid, 0.5)
                v.drain()
                v.tensor_tensor(cmp[:], rw8[:],
                                mid[:, None, :].broadcast_to((128, 128, 8)), ALU.is_gt)
                v.drain()
                v.reduce_sum(cnt2[:], cmp[:].rearrange("p t e -> p e t"), axis=AX.X)
                v.drain().then_inc(s.sem("vf"), 1)
                s.bump("vf")
                v.wait_ge(s.sem("tb"), base["tb"] + it + 1)
                v.tensor_single_scalar(ge, s.pp0[:, 0:8], float(CAP), ALU.is_ge)
                v.drain()
                if s.debug and it in (0, 1, 2):
                    v.tensor_copy(snap[:, it * 24: it * 24 + 8], s.pp0[:, 0:8])
                    v.drain()
                    v.tensor_copy(snap[:, it * 24 + 8: it * 24 + 16], mid)
                    v.drain()
                    v.tensor_copy(snap[:, it * 24 + 16: it * 24 + 24], ge)
                    v.drain()
                tmp = st8[:, 7, :]
                v.tensor_tensor(tmp, mid, lo, ALU.subtract)
                v.drain()
                v.tensor_tensor(tmp, tmp, ge, ALU.mult)
                v.drain()
                v.tensor_tensor(lo_n, lo, tmp, ALU.add)
                v.drain()
                v.tensor_tensor(tmp, hi, mid, ALU.subtract)
                v.drain()
                v.tensor_tensor(tmp, tmp, ge, ALU.mult)
                v.drain()
                v.tensor_tensor(hi_n, mid, tmp, ALU.add)
                v.drain()
            lo_fin = st8[:, 0 + BIS_IT % 2, :]
            v.tensor_copy(s.wgt[:, :, 0:8], s.rw[:, :, 0:8])
            v.drain()
            v.tensor_tensor(cmp[:, 0:16, :], s.rw[:, :, 8:16],
                            lo_fin[:, None, :].broadcast_to((128, 16, 8)), ALU.is_gt)
            v.drain()
            v.tensor_tensor(s.wgt[:, :, 8:16], s.rw[:, :, 8:16], cmp[:, 0:16, :],
                            ALU.mult)
            v.drain().then_inc(s.sem("ve"), 1)
            s.bump("ve")
        block.vector(f_vec)

        if s.debug:
            def f_gp2(g):
                g.wait_ge(s.sem("ve"), s.cnt["ve"])
                g.dma_start(out=s.bis_dbg[:, :], in_=snap[:]
                            ).then_inc(s.sem("so"), 16)
                s.bump("so", 16)
                g.dma_start(out=s.thr_dbg[:, :], in_=st8[:, 0 + BIS_IT % 2, :]
                            ).then_inc(s.sem("so"), 16)
                g.dma_start(out=s.wgt_dbg[:, :],
                            in_=s.wgt[:].rearrange("p a b -> p (a b)")
                            ).then_inc(s.sem("so"), 16)
                s.bump("so", 32)
            block.gpsimd(f_gp2)

    def _attn_items(self):
        items = []
        for i in range(6):
            items.append((0, "full", i, None))
        items.append((0, "diag", (0, 0), 0))
        items.append((0, "diag", (0, 1), 512))
        for i in range(14):
            items.append((1, "full", i, None))
        items.append((1, "diag", (1, 0), 0))
        items.append((1, "diag", (1, 1), 512))
        return items

    def phase_attn(self, block):
        s = self
        nc = s.nc
        sbat = lambda n, off, shape, dt: nc.alloc_sbuf_tensor_at(n, shape, dt, offset=off).ap()
        qh = sbat("at_qh", SLAB1, [128, 16, 1024], BF16)
        ckvi = sbat("at_ckv", SLAB1 + 32 * KB, [128, 2, 4, 512], BF16)
        pT = sbat("at_pt", SLAB1 + 40 * KB, [128, 4, 1024], BF16)
        sums = sbat("at_sums", SLAB1 + 48 * KB, [128, 1024], F32)
        rsr = sbat("at_rsr", SLAB1 + 52 * KB, [128, 2, 1024], F32)
        ctxe = sbat("at_ctxe", SLAB1 + 60 * KB, [128, 2, 1024], BF16)
        wuk = sbat("at_wuk", SLAB2, [128, 4, 2048], BF16)
        wuv = sbat("at_wuv", SLAB2 + 16 * KB, [128, 4, 2048], BF16)
        kT = sbat("at_kt", SLAB2 + 32 * KB, [128, 16, 512], BF16)
        vblk = sbat("at_v", SLAB2 + 48 * KB, [128, 4, 2048], BF16)
        ctx = sbat("at_ctx", SLAB3, [128, 16, 1024], F32)
        base = dict(s.cnt)
        items = self._attn_items()
        UPQ = [s.pp0[:, 0:512], s.pp0[:, 512:1024],
               s.pp0[:, 1024:1536], s.pp0[:, 1536:2048]]
        SS = [s.pp0[:, 0:1024], s.pp0[:, 1024:2048]]
        CTXPS = s.pp1[:, 0:1024]
        XPS = s.pp1[:, 1024:2048]

        NIT = len(items)
        dig = [it for it, (p, knd, _, _) in enumerate(items) if knd == "diag"]

        def f_gp(g):
            for p in range(2):
                if p == 0:
                    g.wait_ge(s.sem("sq"), base["sq"])
                    g.wait_ge(s.sem("sq1"), base["sq1"])
                    g.wait_ge(s.sem("ve"), base["ve"])
                    g.wait_ge(s.sem("so"), base["so"])
                    g.dma_start(out=wuk[:],
                                in_=s.w_ukT.rearrange("(kc p) d -> p kc d", p=128)
                                ).then_inc(s.sem("sw"), 16)
                    s.bump("sw", 16)
                    g.dma_start(out=wuv[:],
                                in_=s.w_uvT.rearrange("(kc p) d -> p kc d", p=128)
                                ).then_inc(s.sem("sw"), 16)
                    s.bump("sw", 16)
                else:
                    g.wait_ge(s.sem("tc"), base["tc"] + 64 * 8)
                g.dma_start(out=qh[:],
                            in_=s.qT_spill[:, :, p * 1024:(p + 1) * 1024]
                            .rearrange("dc p t -> p dc t")
                            ).then_inc(s.sem("sw1"), 16)
                s.bump("sw1", 16)
        block.gpsimd(f_gp)

        def f_sync(sy):
            for it, (p, knd, src, rel) in enumerate(items):
                par = it % 2
                if it < 2:
                    sy.wait_ge(s.sem("ve"), base["ve"])
                if it >= 2:
                    sy.wait_ge(s.sem("td"), base["td"] + 32 * (it - 1))
                if knd == "full":
                    srcap = s.ckv_spill[src]
                else:
                    srcap = s.ckv_own[src[0], src[1]]
                sy.dma_start(out=ckvi[:, par],
                             in_=srcap.rearrange("kc p t -> p kc t")
                             ).then_inc(s.sem("sy" if par == 0 else "sy1"), 16)
                s.bump("sy" if par == 0 else "sy1", 16)
        block.sync(f_sync)

        def f_tens(t):
            t.wait_ge(s.sem("sw"), base["sw"] + 32)
            for it, (p, knd, src, rel) in enumerate(items):
                par = it % 2
                first_of_pass = (it == 0 or it == 8)
                t.wait_ge(s.sem("sy" if par == 0 else "sy1"),
                          base["sy" if par == 0 else "sy1"] + 16 * (it // 2 + 1))
                t.wait_ge(s.sem("sw1"), base["sw1"] + 16 * (p + 1))
                for dc in range(16):
                    q = dc % 4
                    if it > 0 or dc >= 4:
                        pass
                    if it == 0 and dc == 0:
                        t.wait_ge(s.sem("vb"), base["vb"])
                    if dc >= 4:
                        t.wait_ge(s.sem("ad"), base["ad"] + 16 * it + dc - 3)
                    elif it > 0:
                        t.wait_ge(s.sem("ab"), base["ab"] + 64 * it)
                    for kc in range(4):
                        ins = t.matmul(UPQ[q][:, 0:512],
                                       wuk[:, kc, dc * 128:(dc + 1) * 128],
                                       ckvi[:, par, kc, :],
                                       start=(kc == 0), stop=(kc == 3))
                    ins.then_inc(s.sem("td"), 1)
                    s.bump("td")
                for n in range(16):
                    sub, dg = divmod(n, 4)
                    q = n % 4
                    if n >= 4:
                        t.wait_ge(s.sem("va"), base["va"] + 16 * it + n - 3)
                    else:
                        t.wait_ge(s.sem("ad"), base["ad"] + 16 * it + 13 + n)
                    for kc in range(4):
                        ins = t.matmul(UPQ[q][:, 0:512],
                                       ckvi[:, par, kc, sub * 128:(sub + 1) * 128],
                                       wuv[:, kc, dg * 512:(dg + 1) * 512],
                                       start=(kc == 0), stop=(kc == 3))
                    ins.then_inc(s.sem("td"), 1)
                    s.bump("td")
                for n_ in range(65):
                    if n_ < 64:
                        h, kt = divmod(n_, 4)
                        sl = n_ % 2
                        if n_ == 0:
                            t.wait_ge(s.sem("ad"), base["ad"] + 16 * (it + 1))
                            t.wait_ge(s.sem("va"), base["va"] + 16 * (it + 1))
                        prev_user = 64 * it + n_ - 2
                        if prev_user >= 0:
                            t.wait_ge(s.sem("ab"), base["ab"] + prev_user + 1)
                        for half in range(2):
                            ins = t.matmul(SS[sl][:, half * 512:(half + 1) * 512],
                                           kT[:, h, kt * 128:(kt + 1) * 128],
                                           qh[:, h, half * 512:(half + 1) * 512],
                                           start=True, stop=True)
                        ins.then_inc(s.sem("tc"), 1)
                        s.bump("tc")
                    if n_ >= 1:
                        m_ = n_ - 1
                        h, kt = divmod(m_, 4)
                        nn_ = 64 * it + m_
                        if knd == "diag":
                            t.wait_ge(s.sem("vd"), base["vd"] + dig.index(it) * 64
                                      + m_ + 1)
                        else:
                            t.wait_ge(s.sem("ab"), base["ab"] + nn_ + 1)
                        if kt == 0:
                            hh = 16 * it + h
                            if hh > 0:
                                t.wait_ge(s.sem("vc"), base["vc"] + hh)
                        for half in range(2):
                            ins = t.matmul(CTXPS[:, half * 512:(half + 1) * 512],
                                           vblk[:, kt, h * 128:(h + 1) * 128],
                                           pT[:, nn_ % 4, half * 512:(half + 1) * 512],
                                           start=(kt == 0), stop=(kt == 3))
                        if m_ == 0 and it > 0:
                            t.wait_ge(s.sem("ve"), base["ve"] + it)
                        for half in range(2):
                            ins = t.matmul(XPS[:, half * 512:(half + 1) * 512],
                                           s.eh[:, h, :],
                                           pT[:, nn_ % 4, half * 512:(half + 1) * 512],
                                           start=(m_ == 0),
                                           stop=(m_ == 63))
                        ins.then_inc(s.sem("te"), 1)
                        s.bump("te")
        block.tensor(f_tens)

        def f_act(sc):
            for it, (p, knd, src, rel) in enumerate(items):
                for dc in range(16):
                    q = dc % 4
                    sc.wait_ge(s.sem("td"), base["td"] + 32 * it + dc + 1)
                    sc.copy(kT[:, dc, :], UPQ[q][:, 0:512])
                    sc.drain().then_inc(s.sem("ad"), 1)
                    s.bump("ad")
                for h in range(NH):
                    for kt in range(4):
                        nn_ = 64 * it + h * 4 + kt
                        sl = (h * 4 + kt) % 2
                        sc.wait_ge(s.sem("tc"), base["tc"] + nn_ + 1)
                        if nn_ >= 4:
                            sc.wait_ge(s.sem("te"), base["te"] + nn_ - 3)
                        sc.activation(out=pT[:, nn_ % 4, :], in_=SS[sl][:, :],
                                      func=AF.Exp, scale=SCALE,
                                      bias=s.biass[:, it:it + 1])
                        sc.drain().then_inc(s.sem("ab"), 1)
                        s.bump("ab")
        block.scalar(f_act)

        def f_vec(v):
            for it, (p, knd, src, rel) in enumerate(items):
                par = it % 2
                first_of_pass = (it == 0 or it == 8)
                for n in range(16):
                    sub, dg = divmod(n, 4)
                    q = n % 4
                    v.wait_ge(s.sem("td"), base["td"] + 32 * it + 16 + n + 1)
                    v.tensor_copy(vblk[:, sub, dg * 512:(dg + 1) * 512],
                                  UPQ[q][:, 0:512])
                    v.drain().then_inc(s.sem("va"), 1)
                    s.bump("va")
                for h in range(NH):
                    if knd == "diag":
                        for kt in range(4):
                            nn_ = 64 * it + h * 4 + kt
                            inv = rel + kt * 128
                            v.wait_ge(s.sem("ab"), base["ab"] + nn_ + 1)
                            if inv > 0:
                                v.memset(pT[:, nn_ % 4, 0:inv], 0.0)
                                v.drain()
                            v.tensor_tensor(pT[:, nn_ % 4, inv:inv + 128],
                                            pT[:, nn_ % 4, inv:inv + 128],
                                            s.tril[:], ALU.mult)
                            v.drain().then_inc(s.sem("vd"), 1)
                            s.bump("vd")
                    v.wait_ge(s.sem("te"), base["te"] + 64 * it + 4 * h + 4)
                    dst = ctx[:, h, :]
                    if first_of_pass:
                        v.tensor_copy(dst, CTXPS[:, :])
                    else:
                        v.tensor_tensor(dst, dst, CTXPS[:, :], ALU.add)
                    v.drain().then_inc(s.sem("vc"), 1)
                    s.bump("vc")
                v.wait_ge(s.sem("te"), base["te"] + 64 * (it + 1))
                if first_of_pass:
                    v.tensor_copy(sums[:], XPS[:, :])
                else:
                    v.tensor_tensor(sums[:], sums[:], XPS[:, :], ALU.add)
                v.drain().then_inc(s.sem("ve"), 1)
                s.bump("ve")
                if it == 7 or it == 23:
                    v.reciprocal(sums[:], sums[:])
                    v.drain().then_inc(s.sem("vf"), 1)
                    s.bump("vf")
                    for h in range(NH):
                        v.wait_ge(s.sem("so"),
                                  base["so"] + 16 * (33 * p + 2 * h + 2))
                        v.tensor_tensor(ctxe[:, h % 2, :], ctx[:, h, :],
                                        rsr[:, h % 2, :], ALU.mult)
                        v.drain().then_inc(s.sem("vb"), 1)
                        s.bump("vb")
        block.vector(f_vec)

        def f_gp2(g):
            for p in range(2):
                g.wait_ge(s.sem("vf"), base["vf"] + p + 1)
                g.dma_start(out=s.rs_dram[p], in_=sums[0:16, :]
                            ).then_inc(s.sem("so"), 16)
                s.bump("so", 16)
                for h in range(NH):
                    g.wait_ge(s.sem("so"), s.cnt["so"])
                    if h >= 2:
                        g.wait_ge(s.sem("vb"), base["vb"] + 16 * p + h - 1)
                    g.dma_start(out=rsr[:, h % 2, :],
                                in_=s.rs_dram[p, h, :].partition_broadcast(128)
                                ).then_inc(s.sem("so"), 16)
                    s.bump("so", 16)
                    g.wait_ge(s.sem("vb"), base["vb"] + 16 * p + h + 1)
                    g.dma_start(out=s.ctx_spill[p, h],
                                in_=ctxe[:, h % 2, :]).then_inc(s.sem("so"), 16)
                    s.bump("so", 16)
        block.gpsimd(f_gp2)

    def phase_ffn1(self, block):
        s = self
        nc = s.nc
        sbat = lambda n, off, shape, dt: nc.alloc_sbuf_tensor_at(n, shape, dt, offset=off).ap()
        xnT = sbat("f1_xnT", SLAB1, [128, 16, 2048], BF16)
        ug = sbat("f1_ug", SLAB2, [128, 2, 2, 16, 512], BF16)
        ire = sbat("f1_ir", SLAB3, [128, 2, 4, 2048], BF16)
        sil = sbat("f1_sil", SLAB3 + 32 * KB, [128, 2, 512], F32)
        base = dict(s.cnt)

        def f_gp(g):
            g.wait_ge(s.sem("tc"), base["tc"])
            g.wait_ge(s.sem("te"), base["te"])
            for e0 in range(2):
                g.dma_start(out=ug[:, e0, 0],
                            in_=s.uT[:, e0 * EW:(e0 + 1) * EW]
                            .rearrange("(cc p) w -> p cc w", p=128)
                            ).then_inc(s.sem("sw" if e0 == 0 else "sw1"), 16)
                g.dma_start(out=ug[:, e0, 1],
                            in_=s.gT[:, e0 * EW:(e0 + 1) * EW]
                            .rearrange("(cc p) w -> p cc w", p=128)
                            ).then_inc(s.sem("sw" if e0 == 0 else "sw1"), 16)
                s.bump("sw" if e0 == 0 else "sw1", 32)
            g.wait_ge(s.sem("so"), base["so"])
            g.wait_ge(s.sem("vb"), base["vb"])
            g.dma_start(out=xnT[:],
                        in_=s.xn_spill.rearrange("cc p t -> p cc t")
                        ).then_inc(s.sem("so"), 16)
            s.bump("so", 16)
            for e in range(1, NE + 1):
                if 2 <= e < NE:
                    par = e % 2
                    g.wait_ge(s.sem("tc"), base["tc"] + 32 * (e - 1))
                    g.dma_start(out=ug[:, par, 0],
                                in_=s.uT[:, e * EW:(e + 1) * EW]
                                .rearrange("(cc p) w -> p cc w", p=128)
                                ).then_inc(s.sem("sw" if par == 0 else "sw1"), 16)
                    g.dma_start(out=ug[:, par, 1],
                                in_=s.gT[:, e * EW:(e + 1) * EW]
                                .rearrange("(cc p) w -> p cc w", p=128)
                                ).then_inc(s.sem("sw" if par == 0 else "sw1"), 16)
                    s.bump("sw" if par == 0 else "sw1", 32)
                if e >= 1:
                    ep = e - 1
                    parp = ep % 2
                    g.wait_ge(s.sem("va"), base["va"] + 16 * (ep + 1))
                    g.dma_start(out=s.ir_spill[ep].rearrange("wc p t -> p wc t"),
                                in_=ire[:, parp]
                                ).then_inc(s.sem("sk" if parp == 0 else "sk1"), 16)
                    s.bump("sk" if parp == 0 else "sk1", 16)
        block.gpsimd(f_gp)

        def f_tens(t):
            t.wait_ge(s.sem("so"), base["so"] + 16)
            for e in range(NE):
                par = e % 2
                t.wait_ge(s.sem("sw" if par == 0 else "sw1"),
                          base["sw" if par == 0 else "sw1"] + 32 * (e // 2 + 1))
                for n in range(16):
                    tg, wc = divmod(n, 4)
                    nn_ = 32 * e + 2 * n
                    slg = s.pp0[:, (n % 2) * 1024:(n % 2) * 1024 + 512]
                    slu = s.pp1[:, (n % 2) * 1024:(n % 2) * 1024 + 512]
                    if nn_ >= 4:
                        t.wait_ge(s.sem("ab"), base["ab"] + 16 * e + n - 1)
                    for cc in range(16):
                        ins = t.matmul(slg, ug[:, par, 1, cc, wc * 128:(wc + 1) * 128],
                                       xnT[:, cc, tg * 512:(tg + 1) * 512],
                                       start=(cc == 0), stop=(cc == 15))
                    ins.then_inc(s.sem("tc"), 1)
                    s.bump("tc")
                    if nn_ >= 4:
                        t.wait_ge(s.sem("va"), base["va"] + 16 * e + n - 1)
                    for cc in range(16):
                        ins = t.matmul(slu, ug[:, par, 0, cc, wc * 128:(wc + 1) * 128],
                                       xnT[:, cc, tg * 512:(tg + 1) * 512],
                                       start=(cc == 0), stop=(cc == 15))
                    ins.then_inc(s.sem("tc"), 1)
                    s.bump("tc")
        block.tensor(f_tens)

        def f_act(sc):
            for e in range(NE):
                for n in range(16):
                    slg = s.pp0[:, (n % 2) * 1024:(n % 2) * 1024 + 512]
                    sc.wait_ge(s.sem("tc"), base["tc"] + 32 * e + 2 * n + 1)
                    if 16 * e + n >= 2:
                        sc.wait_ge(s.sem("va"), base["va"] + 16 * e + n - 1)
                    sc.activation(out=sil[:, n % 2, :], in_=slg, func=AF.Silu)
                    sc.drain().then_inc(s.sem("ab"), 1)
                    s.bump("ab")
        block.scalar(f_act)

        def f_vec(v):
            for e in range(NE):
                par = e % 2
                for n in range(16):
                    tg, wc = divmod(n, 4)
                    slu = s.pp1[:, (n % 2) * 1024:(n % 2) * 1024 + 512]
                    v.wait_ge(s.sem("tc"), base["tc"] + 32 * e + 2 * n + 2)
                    v.wait_ge(s.sem("ab"), base["ab"] + 16 * e + n + 1)
                    if n == 0 and e >= 2:
                        ps = "sk" if par == 0 else "sk1"
                        v.wait_ge(s.sem(ps), base[ps] + 16 * (e // 2))
                    v.tensor_tensor(ire[:, par, wc, tg * 512:(tg + 1) * 512],
                                    slu, sil[:, n % 2, :], ALU.mult)
                    v.drain().then_inc(s.sem("va"), 1)
                    s.bump("va")
        block.vector(f_vec)

    def phase_ffn2(self, block):
        s = self
        nc = s.nc
        sbat = lambda n, off, shape, dt: nc.alloc_sbuf_tensor_at(n, shape, dt, offset=off).ap()
        acc0 = sbat("f2_acc0", SLAB1, [128, 8, 2048], F32)
        acc1 = sbat("f2_acc1", SLAB3, [128, 8, 2048], F32)
        dte = sbat("f2_dt", SLAB2, [128, 2, 4, 2048], BF16)
        iri = sbat("f2_ir", SLAB2 + 32 * KB, [128, 2, 4, 2048], BF16)
        base = dict(s.cnt)
        self.F2_bufs = dict(acc0=acc0, acc1=acc1)

        def f_gp(g):
            g.wait_ge(s.sem("tc"), base["tc"])
            for e in range(NE):
                par = e % 2
                if e >= 2:
                    g.wait_ge(s.sem("tc"), base["tc"] + 16 * (e - 1))
                g.dma_start(out=dte[:, par],
                            in_=s.dT[e * EW:(e + 1) * EW, :]
                            .rearrange("(wc p) h -> p wc h", p=128)
                            ).then_inc(s.sem("sw" if par == 0 else "sw1"), 16)
                s.bump("sw" if par == 0 else "sw1", 16)
        block.gpsimd(f_gp)

        def f_sync(sy):
            sy.wait_ge(s.sem("tc"), base["tc"])
            sy.wait_ge(s.sem("sk"), base["sk"])
            sy.wait_ge(s.sem("sk1"), base["sk1"])
            for e in range(NE):
                par = e % 2
                if e >= 2:
                    sy.wait_ge(s.sem("tc"), base["tc"] + 16 * (e - 1))
                sy.dma_start(out=iri[:, par],
                             in_=s.ir_spill[e].rearrange("wc p t -> p wc t")
                             ).then_inc(s.sem("sy" if par == 0 else "sy1"), 16)
                s.bump("sy" if par == 0 else "sy1", 16)
        block.sync(f_sync)

        def f_tens(t):
            for e in range(NE):
                par = e % 2
                wsem = "sw" if par == 0 else "sw1"
                ksem = "sy" if par == 0 else "sy1"
                t.wait_ge(s.sem(wsem), base[wsem] + 16 * (e // 2 + 1))
                t.wait_ge(s.sem(ksem), base[ksem] + 16 * (e // 2 + 1))
                for ti in range(16):
                    n = 16 * e + ti
                    ps = s.pp0 if n % 2 == 0 else s.pp1
                    if n >= 2:
                        t.wait_ge(s.sem("va"), base["va"] + n - 1)
                    for hg in range(4):
                        for wc in range(4):
                            ins = t.matmul(ps[:, hg * 512:(hg + 1) * 512],
                                           iri[:, par, wc, ti * 128:(ti + 1) * 128],
                                           dte[:, par, wc, hg * 512:(hg + 1) * 512],
                                           start=(wc == 0), stop=(wc == 3))
                    ins.then_inc(s.sem("tc"), 1)
                    s.bump("tc")
        block.tensor(f_tens)

        def f_vec(v):
            for e in range(NE):
                for ti in range(16):
                    n = 16 * e + ti
                    ps = s.pp0 if n % 2 == 0 else s.pp1
                    acc = acc0 if ti < 8 else acc1
                    sl = acc[:, ti % 8, :]
                    wcol = s.wgt[:, ti, e:e + 1]
                    v.wait_ge(s.sem("tc"), base["tc"] + n + 1)
                    if e == 0:
                        v.tensor_scalar(sl, ps[:, :], wcol, None, ALU.mult)
                    else:
                        v.scalar_tensor_tensor(sl, ps[:, :], wcol, sl,
                                               ALU.mult, ALU.add)
                    v.drain().then_inc(s.sem("va"), 1)
                    s.bump("va")
        block.vector(f_vec)

    def phase_final(self, block):
        s = self
        nc = s.nc
        sbat = lambda n, off, shape, dt: nc.alloc_sbuf_tensor_at(n, shape, dt, offset=off).ap()
        wo = sbat("fn_wo", SLAB2, [128, 16, 1024], BF16)
        ctxt = sbat("fn_ctx", SLAB2 + 32 * KB, [128, 2, 16, 128], BF16)
        xfin = sbat("fn_x", SLAB2 + 40 * KB, [128, 2, 1024], F32)
        outt = sbat("fn_out", SLAB2 + 48 * KB, [128, 2, 1024], F32)
        outt2 = sbat("fn_out2", SLAB2 + 56 * KB, [128, 2, 1024], F32)
        bo = sbat("fn_bo", OFF_EH, [1, 2048], BF16)
        acc0 = self.F2_bufs["acc0"]
        acc1 = self.F2_bufs["acc1"]
        base = dict(s.cnt)

        def f_gp(g):
            g.wait_ge(s.sem("te"), base["te"])
            g.wait_ge(s.sem("tc"), base["tc"])
            g.dma_start(out=bo[:], in_=s.b_o[:, :]).then_inc(s.sem("sw"), 16)
            s.bump("sw", 16)
            for ch in range(2):
                if ch == 1:
                    g.wait_ge(s.sem("tc"), base["tc"] + 16)
                g.dma_start(out=wo[:],
                            in_=s.w_oT.rearrange("(dc p) c -> p dc c", p=128)
                            [:, :, ch * 1024:(ch + 1) * 1024]
                            ).then_inc(s.sem("sw1"), 16)
                s.bump("sw1", 16)
        block.gpsimd(f_gp)

        def f_act(sc):
            sc.wait_ge(s.sem("va"), base["va"])
            for ti in range(16):
                acc = acc0 if ti < 8 else acc1
                sc.activation(out=acc[:, ti % 8, :], in_=acc[:, ti % 8, :],
                              func=AF.Silu)
                sc.drain().then_inc(s.sem("ab"), 1)
                s.bump("ab")
        block.scalar(f_act)

        def emit_out_dma(sy, m):
            mpar = m % 2
            ch_, ti_ = divmod(m, 16)
            sy.wait_ge(s.sem("vb"), base["vb"] + m + 1)
            sy.dma_start(out=s.out[ti_ * 128:(ti_ + 1) * 128,
                                   ch_ * 1024:(ch_ + 1) * 1024],
                         in_=outt2[:, mpar, :]
                         ).then_inc(s.sem("sy" if mpar == 0 else "sy1"), 16)
            s.bump("sy" if mpar == 0 else "sy1", 16)

        def f_sync(sy):
            sy.wait_ge(s.sem("tc"), base["tc"])
            for ch in range(2):
                for ti in range(16):
                    n = ch * 16 + ti
                    par = n % 2
                    if n >= 2:
                        emit_out_dma(sy, n - 2)
                        sy.wait_ge(s.sem("tc"), base["tc"] + n - 1)
                    p = ti // 8
                    sy.dma_start(out=ctxt[:, par],
                                 in_=s.ctx_spill[p, :, :, (ti % 8) * 128:
                                                 (ti % 8) * 128 + 128]
                                 .rearrange("dc p t -> p dc t")
                                 ).then_inc(s.sem("sq" if par == 0 else "sq1"), 16)
                    s.bump("sq" if par == 0 else "sq1", 16)
                    if n >= 2:
                        sy.wait_ge(s.sem("vb"), base["vb"] + n - 1)
                    sy.dma_start(out=xfin[:, par, :],
                                 in_=s.x_own[ti * 128:(ti + 1) * 128,
                                             ch * 1024:(ch + 1) * 1024]
                                 ).then_inc(s.sem("sx" if par == 0 else "sx1"), 16)
                    s.bump("sx" if par == 0 else "sx1", 16)
            emit_out_dma(sy, 30)
            emit_out_dma(sy, 31)
        block.sync(f_sync)

        def f_tens(t):
            t.wait_ge(s.sem("sw"), base["sw"] + 16)
            t.wait_ge(s.sem("va"), base["va"])
            for ch in range(2):
                t.wait_ge(s.sem("sw1"), base["sw1"] + 16 * (ch + 1))
                for ti in range(16):
                    n = ch * 16 + ti
                    par = n % 2
                    ps = (s.pp0 if par == 0 else s.pp1)[:, 0:1024]
                    t.wait_ge(s.sem("sq" if par == 0 else "sq1"),
                              base["sq" if par == 0 else "sq1"] + 16 * (n // 2 + 1))
                    if n >= 2:
                        t.wait_ge(s.sem("vb"), base["vb"] + n - 1)
                    for dc in range(16):
                        for half in range(2):
                            t.matmul(ps[:, half * 512:(half + 1) * 512],
                                     ctxt[:, par, dc, :],
                                     wo[:, dc, half * 512:(half + 1) * 512],
                                     start=(dc == 0), stop=False)
                    for half in range(2):
                        ins = t.matmul(ps[:, half * 512:(half + 1) * 512],
                                       s.ones_row[:],
                                       bo[0:1, ch * 1024 + half * 512:
                                          ch * 1024 + (half + 1) * 512],
                                       start=False, stop=True)
                    ins.then_inc(s.sem("tc"), 1)
                    s.bump("tc")
        block.tensor(f_tens)

        def f_vec(v):
            for ch in range(2):
                for ti in range(16):
                    n = ch * 16 + ti
                    par = n % 2
                    ps = (s.pp0 if par == 0 else s.pp1)[:, 0:1024]
                    acc = acc0 if ti < 8 else acc1
                    gsl = acc[:, ti % 8, ch * 1024:(ch + 1) * 1024]
                    v.wait_ge(s.sem("tc"), base["tc"] + n + 1)
                    v.wait_ge(s.sem("ab"), base["ab"] + ti + 1)
                    v.wait_ge(s.sem("sx" if par == 0 else "sx1"),
                              base["sx" if par == 0 else "sx1"]
                              + 16 * (n // 2 + 1))
                    if n >= 2:
                        ksem = "sy" if par == 0 else "sy1"
                        v.wait_ge(s.sem(ksem), base[ksem] + 16 * (n // 2))
                    v.scalar_tensor_tensor(outt[:, par, :], ps, 1.0, gsl,
                                           ALU.mult, ALU.mult)
                    v.drain()
                    v.scalar_tensor_tensor(outt2[:, par, :], outt[:, par, :],
                                           1.0, xfin[:, par, :],
                                           ALU.mult, ALU.add)
                    v.drain().then_inc(s.sem("vb"), 1)
                    s.bump("vb")
        block.vector(f_vec)

        def f_gp2(g):
            g.wait_ge(s.sem("sy"), s.cnt["sy"])
            g.wait_ge(s.sem("sy1"), s.cnt["sy1"])
        block.gpsimd(f_gp2)

    def build(self, upto="final"):
        nc = self.nc
        order = ["prepare", "A", "B", "thresh", "attn", "ffn1", "ffn2", "final"]
        fns = dict(prepare=self.phase_prepare, A=self.phase_A, B=self.phase_B,
                   thresh=self.phase_thresh, attn=self.phase_attn,
                   ffn1=self.phase_ffn1, ffn2=self.phase_ffn2,
                   final=self.phase_final)
        with nc.Block() as block:
            for name in order:
                fns[name](block)
                if name == upto:
                    break
            if upto != "final":
                self.phase_epilogue(block)
        return nc

    def phase_epilogue(self, block):
        s = self

        def f_gp(g):
            g.wait_ge(s.sem("cc"), s.cnt["cc"])
            g.wait_ge(s.sem("so"), s.cnt["so"])
            g.wait_ge(s.sem("sk"), s.cnt["sk"])
            g.wait_ge(s.sem("ve"), s.cnt["ve"])
            if s.debug:
                g.dma_start(out=s.log_dbg[:, :],
                            in_=s.log[:].rearrange("p a b -> p (a b)")
                            ).then_inc(s.sem("so"), 16)
                g.dma_start(out=s.rw_dbg[:, :],
                            in_=s.rw[:].rearrange("p a b -> p (a b)")
                            ).then_inc(s.sem("so"), 16)
                s.bump("so", 32)
            g.dma_start(out=s.out[:, :], in_=s.x_own[:, :]).then_inc(s.sem("so"), 16)
            s.bump("so", 16)
            g.wait_ge(s.sem("so"), s.cnt["so"])
        block.gpsimd(f_gp)


def _stage_inputs(inputs):
    x = np.asarray(inputs["L0_attn_x"])
    T = lambda a: np.ascontiguousarray(np.asarray(a).T)
    w = {
        "w_dkvT": T(inputs["L0_attn_W_DKV"]),
        "w_dqT": T(inputs["L0_attn_W_DQ"]),
        "w_uqT": T(inputs["L0_attn_W_UQ"]),
        "w_ukT": np.ascontiguousarray(np.asarray(inputs["L0_attn_W_UK"]).T),
        "w_uvT": np.ascontiguousarray(np.asarray(inputs["L0_attn_W_UV"]).T),
        "w_oT": T(inputs["L0_attn_W_O"]),
        "b_o": np.asarray(inputs["L0_attn_b_O"]).reshape(1, D),
        "gate_wT": T(inputs["L0_ffn_gate_w"]),
        "uT": T(inputs["L0_ffn_u"]),
        "gT": T(inputs["L0_ffn_g"]),
        "dT": T(inputs["L0_ffn_d"]),
    }
    tril = np.triu(np.ones((128, 128), dtype=np.float32))
    ident = np.eye(128, dtype=np.float32)
    in_maps = []
    for core in range(8):
        b, j = divmod(core, 4)
        xs = np.ascontiguousarray(x[b])
        c1 = x[b, j * 1024:(j + 1) * 1024]
        c2 = x[b, (7 - j) * 1024:(8 - j) * 1024]
        xo = np.ascontiguousarray(np.concatenate([c1, c2], axis=0))
        bias = np.zeros((128, 32), dtype=np.float32)
        for it in range(6):
            if it >= 2 * j:
                bias[:, it] = NEGB
        for it in range(14):
            if it >= 2 * (7 - j):
                bias[:, 8 + it] = NEGB
        m = dict(w)
        m.update({
            "x_seq": xs, "x_own": xo,
            "tril_in": tril, "bias_in": bias, "ident_in": ident,
        })
        in_maps.append(m)
    return in_maps


def kernel(**inputs):
    in_maps = _stage_inputs(inputs)
    mk = MK(debug=False)
    nc = mk.build()
    res = run_bass_kernel_spmd(nc, in_maps, core_ids=list(range(8)))
    x = np.asarray(inputs["L0_attn_x"])
    out = np.empty((B, SEQ, D), dtype=np.float32)
    for core in range(8):
        b, j = divmod(core, 4)
        o = res.results[core]["out"]
        out[b, j * 1024:(j + 1) * 1024] = o[0:1024]
        out[b, (7 - j) * 1024:(8 - j) * 1024] = o[1024:2048]
    return out


# version 14 (final)
# speedup vs baseline: 1.0180x; 1.0180x over previous
import math
import numpy as np
import ml_dtypes
import concourse.bass as bass
import concourse.mybir as mybir
from concourse.bass_utils import run_bass_kernel_spmd

F32 = mybir.dt.float32
BF16 = mybir.dt.bfloat16
AF = mybir.ActivationFunctionType
ALU = mybir.AluOpType
AX = mybir.AxisListType

B, SEQ, D = 2, 8192, 2048
DKV, NH, DH = 512, 16, 128
NE, EW = 16, 512
NTOK = 2048
NT_OWN = 16
NT_SEQ = 64
NT_ALL = NT_SEQ + NT_OWN
SCALE = 1.0 / math.sqrt(DH)
EPS = 1e-5
NEGB = -1e30
CAP = 1024
BIS_IT = 28

KB = 1024
SBB = 16512
OFF_TRIL = SBB + 0
OFF_BIAS = SBB + 512
OFF_EH = SBB + 640
OFF_RW = SBB + 4736
OFF_WGT = SBB + 5760
OFF_LOG = SBB + 6784
OFF_SMALL = SBB + 7808
OFF_ONES = SBB + 8448
OFF_IDF = SBB + 8704
OFF_IDB = SBB + 9216
SLAB1 = SBB + 12288
SLAB2 = SLAB1 + 64 * KB
SLAB3 = SLAB2 + 64 * KB
SLAB_END = SLAB3 + 64 * KB
assert SLAB_END <= 229344


class MK:
    def __init__(self, debug=False, coll=True):
        self.debug = debug
        self.coll = coll
        nc = bass.Bass("TRN2", target_bir_lowering=False)
        self.nc = nc
        dbg = "ExternalOutput" if debug else "Internal"

        di = lambda n, s: nc.dram_tensor(n, s, F32, kind="ExternalInput").ap()
        self.x_seq = di("x_seq", [SEQ, D])
        self.x_own = di("x_own", [NTOK, D])
        self.w_dkvT = di("w_dkvT", [D, DKV])
        self.w_dqT = di("w_dqT", [D, D])
        self.w_uqT = di("w_uqT", [D, D])
        self.w_ukT = di("w_ukT", [DKV, D])
        self.w_uvT = di("w_uvT", [DKV, D])
        self.w_oT = di("w_oT", [D, D])
        self.b_o = di("b_o", [1, D])
        self.gate_wT = di("gate_wT", [D, NE])
        self.uT = di("uT", [D, NE * EW])
        self.gT = di("gT", [D, NE * EW])
        self.dT = di("dT", [NE * EW, D])
        self.tril_in = di("tril_in", [128, 128])
        self.bias_in = di("bias_in", [128, 32])
        self.ident_in = di("ident_in", [128, 128])
        self.out = nc.dram_tensor("out", [NTOK, D], F32, kind="ExternalOutput").ap()

        dr = lambda n, s, dt, kind="Internal": nc.dram_tensor(n, s, dt, kind=kind).ap()
        self.ckv_spill = dr("ckv_spill", [16, 4, 128, 512], BF16, dbg)
        self.ckv_own = dr("ckv_own", [2, 2, 4, 128, 512], BF16, dbg)
        self.xn_spill = dr("xn_spill", [16, 128, NTOK], BF16, dbg)
        self.qT_spill = dr("qT_spill", [16, 128, NTOK], BF16, dbg)
        self.ctx_spill = dr("ctx_spill", [2, 16, 128, 1024], BF16, dbg)
        self.ir_spill = dr("ir_spill", [16, 4, 128, NTOK], BF16, dbg)
        self.rs_dram = dr("rs_dram", [2, NH, 1024], F32, dbg)
        self.cnt_dram = dr("cnt_dram", [128, 8], F32)
        self.lg_local = dr("lg_local", [NTOK, NE], F32)
        self.lg_all = dr("lg_all", [8 * NTOK, NE], F32)
        if debug:
            self.bis_dbg = dr("bis_dbg", [128, 96], F32, "ExternalOutput")
            self.thr_dbg = dr("thr_dbg", [128, 8], F32, "ExternalOutput")
            self.wgt_dbg = dr("wgt_dbg", [128, 256], F32, "ExternalOutput")
            self.log_dbg = dr("log_dbg", [128, 256], F32, "ExternalOutput")
            self.rw_dbg = dr("rw_dbg", [128, 256], F32, "ExternalOutput")

        sb = lambda n, off, shape, dt: nc.alloc_sbuf_tensor_at(n, shape, dt, offset=off).ap()
        self.tril = sb("s_tril", OFF_TRIL, [128, 128], F32)
        self.biass = sb("s_bias", OFF_BIAS, [128, 32], F32)
        self.eh = sb("s_eh", OFF_EH, [128, NH, 128], BF16)
        self.rw = sb("s_rw", OFF_RW, [128, 16, 16], F32)
        self.wgt = sb("s_wgt", OFF_WGT, [128, 16, 16], F32)
        self.log = sb("s_log", OFF_LOG, [128, 16, 16], F32)
        self.small = sb("s_small", OFF_SMALL, [128, 160], F32)
        self.ones_row = sb("s_ones", OFF_ONES, [1, 128], BF16)
        self.idf = sb("s_idf", OFF_IDF, [128, 128], F32)
        self.idb = sb("s_idb", OFF_IDB, [128, 128], BF16)

        self.pp0 = nc.alloc_psum_tensor("pp0", [128, 2048], F32).ap()
        self.pp1 = nc.alloc_psum_tensor("pp1", [128, 2048], F32).ap()

        self.sems = {}
        for name in ["sx", "sx1", "sx2", "sx3",
                     "sw", "sw1", "sk", "sk1", "sq", "sq1", "so",
                     "sy", "sy1",
                     "va", "vb", "vc", "vd", "ve", "vf",
                     "aa", "ab", "ac_", "ad",
                     "ta", "tb", "tc", "td", "te",
                     "cc"]:
            self.sems[name] = nc.alloc_semaphore(name)
        self.cnt = {k: 0 for k in self.sems}

    def sem(self, n):
        return self.sems[n]

    def bump(self, n, k=1):
        self.cnt[n] += k
        return self.cnt[n]

    def sm(self, slot, par=0):
        return self.small[:, slot + par: slot + par + 1]

    def phase_prepare(self, block):
        s = self

        def f_sync(sy):
            sy.dma_start(out=s.tril[:], in_=s.tril_in[:]).then_inc(s.sem("sx"), 16)
            sy.dma_start(out=s.biass[:], in_=s.bias_in[:]).then_inc(s.sem("sx"), 16)
            sy.dma_start(out=s.idf[:], in_=s.ident_in[:]).then_inc(s.sem("sx"), 16)
            s.bump("sx", 48)
        block.sync(f_sync)

        def f_gp(g):
            g.dma_start(out=s.idb[:], in_=s.ident_in[:]).then_inc(s.sem("sw"), 16)
            s.bump("sw", 16)
        block.gpsimd(f_gp)

        def f_vec(v):
            v.memset(s.eh[:], 0.0)
            v.drain()
            for h in range(NH):
                v.memset(s.eh[:, h, h: h + 1], 1.0)
            v.memset(s.ones_row[:], 1.0)
            v.drain().then_inc(s.sem("va"), 1)
            s.bump("va")
        block.vector(f_vec)

    def phase_A(self, block):
        s = self
        nc = s.nc
        NT = NT_ALL

        sbat = lambda n, off, shape, dt: nc.alloc_sbuf_tensor_at(n, shape, dt, offset=off).ap()
        A2 = SLAB2
        wdkv = sbat("a_wdkv", A2, [128, 16, 512], BF16)
        gwt = sbat("a_gwt", A2 + 16 * KB, [128, 16, 16], F32)
        xtile = sbat("a_x", A2 + 17 * KB, [128, 4, D], F32)
        xnf32 = sbat("a_xnf32", A2 + 49 * KB, [128, 2, D], F32)
        A3 = SLAB3
        xnTg = sbat("a_xnTg", A3, [128, 16, 512], BF16)
        xnTf = sbat("a_xnTf", A3 + 16 * KB, [128, 2, 16, 128], F32)
        ckvo = sbat("a_ckvo", A3 + 32 * KB, [128, 4, 512], BF16)
        msk1 = sbat("a_msk1", A3 + 36 * KB, [128, 16, 16], F32)
        msk2 = sbat("a_msk2", A3 + 37 * KB, [128, 16, 16], F32)
        sqs = sbat("a_sq", A3 + 38 * KB, [128, D], F32)
        xnbf = sbat("a_xnbf", A3 + 46 * KB, [128, 4, D], BF16)
        xnT_own = sbat("xnT_own", SLAB1, [128, 16, NTOK], BF16)
        self.wB_early = sbat("b_w", SLAB2, [128, 16, 2048], BF16)
        self.A_bufs = dict(xnT_own=xnT_own)

        def x_src(i):
            if i < NT_SEQ:
                return s.x_seq[i * 128:(i + 1) * 128, :]
            k = i - NT_SEQ
            return s.x_own[k * 128:(k + 1) * 128, :]

        base = dict(s.cnt)

        SX = ["sx", "sx1", "sx2", "sx3"]
        def f_sync(sy):
            for i in range(NT):
                if i >= 4:
                    sy.wait_ge(s.sem("ac_"), base["ac_"] + i - 3)
                sy.dma_start(out=xtile[:, i % 4], in_=x_src(i)
                             ).then_inc(s.sem(SX[i % 4]), 16)
                s.bump(SX[i % 4], 16)
        block.sync(f_sync)

        def f_gp(g):
            g.dma_start(out=wdkv[:],
                        in_=s.w_dkvT.rearrange("(cc p) kv -> p cc kv", p=128)
                        ).then_inc(s.sem("sw"), 16)
            g.dma_start(out=gwt[:],
                        in_=s.gate_wT.rearrange("(cc p) e -> p cc e", p=128)
                        ).then_inc(s.sem("sw"), 16)
            s.bump("sw", 32)
            for gi in range(20):
                g.wait_ge(s.sem("ad"), base["ad"] + 4 * (gi + 1))
                if gi < 16:
                    dst = s.ckv_spill[gi].rearrange("kc p t -> p kc t")
                else:
                    k = gi - 16
                    dst = s.ckv_own[k // 2, k % 2].rearrange("kc p t -> p kc t")
                g.dma_start(out=dst, in_=ckvo[:]).then_inc(s.sem("sk"), 16)
                s.bump("sk", 16)
            for k in range(NT_OWN):
                g.wait_ge(s.sem("ve"), base["ve"] + k + 1)
                g.dma_start(out=s.lg_local[k * 128:(k + 1) * 128, :],
                            in_=s.log[:, k, :]).then_inc(s.sem("so"), 16)
                s.bump("so", 16)
            g.wait_ge(s.sem("so"), s.cnt["so"])
            if s.coll:
                g.collective_compute("AllGather", ALU.bypass,
                                     replica_groups=[list(range(8))],
                                     ins=[s.lg_local.opt()],
                                     outs=[s.lg_all.opt()]).then_inc(s.sem("cc"))
                s.bump("cc", 1)
            else:
                g.dma_start(out=s.lg_all[0:NTOK, :], in_=s.lg_local[:, :]
                            ).then_inc(s.sem("cc"), 16)
                s.bump("cc", 16)
            g.wait_ge(s.sem("ac_"), base["ac_"] + NT)
            g.wait_ge(s.sem("ta"), base["ta"] + NT)
            g.wait_ge(s.sem("tc"), base["tc"] + NT_OWN)
            g.wait_ge(s.sem("tb"), base["tb"] + 20)
            g.dma_start(out=self.wB_early,
                        in_=s.w_dqT.rearrange("(cc p) q -> p cc q", p=128)
                        ).then_inc(s.sem("sw"), 16)
            s.bump("sw", 16)
            g.wait_ge(s.sem("va"), base["va"] + NT)
            g.dma_start(out=s.xn_spill.rearrange("cc p t -> p cc t"),
                        in_=xnT_own[:]).then_inc(s.sem("so"), 16)
            s.bump("so", 16)
        block.gpsimd(f_gp)

        def f_vec(v):
            def V1(i):
                q4 = i % 4
                v.wait_ge(s.sem(SX[q4]), base[SX[q4]] + 16 * (i // 4 + 1))
                v.reduce_sum(s.sm(0, 16 * q4), xtile[:, q4], axis=AX.X)
                v.drain()
                v.tensor_scalar_mul(s.sm(0, 16 * q4), s.sm(0, 16 * q4), -1.0 / D)
                v.drain().then_inc(s.sem("vb"), 1)
                s.bump("vb")

            def V2(i):
                q4 = i % 4
                v.wait_ge(s.sem("aa"), base["aa"] + i + 1)
                v.tensor_scalar(s.sm(4, 16 * q4), s.sm(2, 16 * q4), 1.0 / D, EPS,
                                ALU.mult, ALU.add)
                v.drain().then_inc(s.sem("vc"), 1)
                s.bump("vc")

            def V3(i):
                q4 = i % 4
                v.wait_ge(s.sem("ab"), base["ab"] + i + 1)
                v.reciprocal(s.sm(8, 16 * q4), s.sm(6, 16 * q4))
                v.drain()
                v.tensor_tensor(s.sm(12, 16 * q4), s.sm(8, 16 * q4),
                                s.sm(8, 16 * q4), ALU.mult)
                v.drain()
                v.tensor_tensor(s.sm(12, 16 * q4), s.sm(12, 16 * q4),
                                s.sm(4, 16 * q4), ALU.mult)
                v.drain()
                v.tensor_scalar(s.sm(12, 16 * q4), s.sm(12, 16 * q4), -0.5, 1.5,
                                ALU.mult, ALU.add)
                v.drain()
                v.tensor_tensor(s.sm(8, 16 * q4), s.sm(8, 16 * q4),
                                s.sm(12, 16 * q4), ALU.mult)
                v.drain()
                v.tensor_tensor(s.sm(10, 16 * q4), s.sm(0, 16 * q4),
                                s.sm(8, 16 * q4), ALU.mult)
                v.drain().then_inc(s.sem("vd"), 1)
                s.bump("vd")

            def V4(i):
                own = i >= NT_SEQ
                k = i - NT_SEQ
                gi = i // 4
                v.wait_ge(s.sem("ta"), base["ta"] + i + 1)
                if not own:
                    if gi >= 1:
                        v.wait_ge(s.sem("tb"), base["tb"] + gi)
                    srcb = s.pp0.bitcast(BF16)
                    half = (i % 2) * 2048
                    v.tensor_copy(
                        xnTg[:, :, (i % 4) * 128:(i % 4) * 128 + 128],
                        srcb[:, half:half + 2048]
                            .rearrange("p (cc t) -> p cc t", t=128))
                    v.drain().then_inc(s.sem("va"), 1)
                    s.bump("va")
                else:
                    if k >= 2:
                        v.wait_ge(s.sem("tc"), base["tc"] + k - 1)
                    v.tensor_copy(xnTf[:, k % 2], s.pp0[:, :]
                                  .rearrange("p (cc t) -> p cc t", t=128))
                    v.drain()
                    v.tensor_copy(xnT_own[:, :, k * 128:(k + 1) * 128],
                                  xnTf[:, k % 2])
                    v.drain().then_inc(s.sem("va"), 1)
                    s.bump("va")
                    v.wait_ge(s.sem("tc"), base["tc"] + k + 1)
                    v.tensor_copy(s.log[:, k, :], s.pp1[:, 0:16])
                    v.drain().then_inc(s.sem("ve"), 1)
                    s.bump("ve")

            for i in range(NT + 3):
                if i < NT:
                    V1(i)
                if 1 <= i < NT + 1:
                    V2(i - 1)
                if 2 <= i < NT + 2:
                    V3(i - 2)
                if 3 <= i < NT + 3:
                    V4(i - 3)
            L = s.log[:]
            m1 = s.small[:, 16:32]
            m2 = s.small[:, 32:48]
            dd = s.small[:, 48:64]
            w1 = s.small[:, 64:80]
            w2 = s.small[:, 80:96]
            v.reduce_max(m1, L, axis=AX.X)
            v.drain()
            v.tensor_tensor(msk1[:], L, m1[:, :, None].broadcast_to((128, 16, 16)),
                            ALU.is_equal)
            v.drain()
            v.scalar_tensor_tensor(msk2[:], msk1[:], -1e38, L, ALU.mult, ALU.add)
            v.drain()
            v.reduce_max(m2, msk2[:], axis=AX.X)
            v.drain()
            v.tensor_tensor(msk2[:], msk2[:],
                            m2[:, :, None].broadcast_to((128, 16, 16)), ALU.is_equal)
            v.drain()
            v.tensor_tensor(dd, m1, m2, ALU.subtract)
            v.drain().then_inc(s.sem("vb"), 1)
            s.bump("vb")
            v.wait_ge(s.sem("ab"), base["ab"] + NT + 1)
            v.tensor_scalar(w2, w1, -1.0, 1.0, ALU.mult, ALU.add)
            v.drain()
            v.tensor_tensor(s.rw[:], msk1[:],
                            w1[:, :, None].broadcast_to((128, 16, 16)), ALU.mult)
            v.drain()
            v.tensor_tensor(msk1[:], msk2[:],
                            w2[:, :, None].broadcast_to((128, 16, 16)), ALU.mult)
            v.drain()
            v.tensor_tensor(s.rw[:], s.rw[:], msk1[:], ALU.add)
            v.drain().then_inc(s.sem("ve"), 1)
            s.bump("ve")
        block.vector(f_vec)

        def f_act(sc):
            def A1(i):
                q4 = i % 4
                sc.wait_ge(s.sem("vb"), base["vb"] + i + 1)
                sc.activation(out=sqs[:], in_=xtile[:, q4], func=AF.Square,
                              bias=s.sm(0, 16 * q4), scale=1.0,
                              accum_out=s.sm(2, 16 * q4))
                sc.drain().then_inc(s.sem("aa"), 1)
                s.bump("aa")

            def A2(i):
                q4 = i % 4
                sc.wait_ge(s.sem("vc"), base["vc"] + i + 1)
                sc.sqrt(s.sm(6, 16 * q4), s.sm(4, 16 * q4))
                sc.drain().then_inc(s.sem("ab"), 1)
                s.bump("ab")

            def A3(i):
                own = i >= NT_SEQ
                q4 = i % 4
                par = i % 2
                sc.wait_ge(s.sem("vd"), base["vd"] + i + 1)
                if not own and i >= 4:
                    sc.wait_ge(s.sem("ta"), base["ta"] + i - 3)
                if own and i >= 2:
                    sc.wait_ge(s.sem("ta"), base["ta"] + i - 1)
                dst = xnbf[:, q4] if not own else xnf32[:, par]
                sc.activation(out=dst, in_=xtile[:, q4], func=AF.Identity,
                              bias=s.sm(10, 16 * q4), scale=s.sm(8, 16 * q4))
                sc.drain().then_inc(s.sem("ac_"), 1)
                s.bump("ac_")

            def ACKV(i):
                gi = i // 4
                if gi >= 1:
                    sc.wait_ge(s.sem("sk"), base["sk"] + 16 * gi)
                for kc in range(4):
                    sc.wait_ge(s.sem("td"), base["td"] + 4 * gi + kc + 1)
                    sc.copy(ckvo[:, kc, :], s.pp1[:, kc * 512:(kc + 1) * 512])
                    sc.drain().then_inc(s.sem("ad"), 1)
                    s.bump("ad")

            for i in range(NT + 2):
                if i < NT:
                    A1(i)
                if 1 <= i < NT + 1:
                    A2(i - 1)
                if 2 <= i < NT + 2:
                    A3(i - 2)
                    if (i - 2) % 4 == 3:
                        ACKV(i - 2)
            sc.wait_ge(s.sem("vb"), base["vb"] + NT + 1)
            sc.activation(out=s.small[:, 64:80], in_=s.small[:, 48:64],
                          func=AF.Sigmoid)
            sc.drain().then_inc(s.sem("ab"), 1)
            s.bump("ab")
        block.scalar(f_act)

        def f_tens(t):
            t.wait_ge(s.sem("sw"), base["sw"] + 32)
            for i in range(NT):
                own = i >= NT_SEQ
                k = i - NT_SEQ
                par = i % 2
                gi = i // 4
                t.wait_ge(s.sem("ac_"), base["ac_"] + i + 1)
                if not own:
                    if i >= 2:
                        t.wait_ge(s.sem("va"), base["va"] + i - 1)
                    dstb = s.pp0.bitcast(BF16)
                    half = par * 2048
                    for cc in range(16):
                        ins = t.transpose(
                            dstb[:, half + cc * 128: half + (cc + 1) * 128],
                            xnbf[:, i % 4][:, cc * 128:(cc + 1) * 128], s.idb[:])
                    ins.then_inc(s.sem("ta"), 1)
                    s.bump("ta")
                else:
                    t.wait_ge(s.sem("va"), base["va"] + i)
                    for cc in range(16):
                        ins = t.transpose(s.pp0[:, cc * 128:(cc + 1) * 128],
                                          xnf32[:, par][:, cc * 128:(cc + 1) * 128],
                                          s.idf[:])
                    ins.then_inc(s.sem("ta"), 1)
                    s.bump("ta")
                    t.wait_ge(s.sem("va"), base["va"] + i + 1)
                    if k % 4 == 0:
                        t.wait_ge(s.sem("ad"), base["ad"] + 4 * gi)
                    for cc in range(16):
                        ins = t.matmul(s.pp1[:, 0:16], xnTf[:, k % 2][:, cc, :],
                                       gwt[:, cc, :], start=(cc == 0), stop=(cc == 15))
                    ins.then_inc(s.sem("tc"), 1)
                    s.bump("tc")
                if i % 4 == 3:
                    t.wait_ge(s.sem("va"), base["va"] + i + 1)
                    if own:
                        t.wait_ge(s.sem("ve"), base["ve"] + k + 1)
                    if gi >= 1:
                        t.wait_ge(s.sem("ad"), base["ad"] + 4 * gi)
                    for kc in range(4):
                        for cc in range(16):
                            if own:
                                k0 = k - 3
                                rhs = xnT_own[:, cc, k0 * 128: k0 * 128 + 512]
                            else:
                                rhs = xnTg[:, cc, :]
                            ins = t.matmul(s.pp1[:, kc * 512:(kc + 1) * 512],
                                           wdkv[:, cc, kc * 128:(kc + 1) * 128],
                                           rhs, start=(cc == 0), stop=(cc == 15))
                        ins.then_inc(s.sem("td"), 1)
                        s.bump("td")
                    t.drain().then_inc(s.sem("tb"), 1)
                    s.bump("tb")
        block.tensor(f_tens)

    def phase_B(self, block):
        s = self
        nc = s.nc
        sbat = lambda n, off, shape, dt: nc.alloc_sbuf_tensor_at(n, shape, dt, offset=off).ap()
        wB = self.wB_early
        cqT = sbat("b_cq", SLAB3, [128, 16, 2048], BF16)
        qev = sbat("b_qev", SLAB1, [128, 2, 512], BF16)
        xnT_own = self.A_bufs["xnT_own"]
        base = dict(s.cnt)
        PS = [s.pp0[:, 0:512], s.pp0[:, 512:1024],
              s.pp0[:, 1024:1536], s.pp0[:, 1536:2048]]

        def f_gp(g):
            g.wait_ge(s.sem("ta"), base["ta"] + 64)
            g.dma_start(out=wB[:],
                        in_=s.w_uqT.rearrange("(cc p) q -> p cc q", p=128)
                        ).then_inc(s.sem("sw"), 16)
            s.bump("sw", 16)
        block.gpsimd(f_gp)

        def f_tens(t):
            t.wait_ge(s.sem("sw"), base["sw"])
            t.wait_ge(s.sem("va"), base["va"])
            for n in range(64):
                g_, qc = divmod(n, 16)
                if n >= 4:
                    t.wait_ge(s.sem("va"), base["va"] + n - 3)
                for cc in range(16):
                    ins = t.matmul(PS[n % 4], wB[:, cc, qc * 128:(qc + 1) * 128],
                                   xnT_own[:, cc, g_ * 512:(g_ + 1) * 512],
                                   start=(cc == 0), stop=(cc == 15))
                ins.then_inc(s.sem("ta"), 1)
                s.bump("ta")
            t.wait_ge(s.sem("sw"), base["sw"] + 16)
            for n in range(64):
                g_, qc = divmod(n, 16)
                if n < 4:
                    t.wait_ge(s.sem("va"), base["va"] + 64)
                else:
                    t.wait_ge(s.sem("vb"), base["vb"] + n - 3)
                for cc in range(16):
                    ins = t.matmul(PS[n % 4], wB[:, cc, qc * 128:(qc + 1) * 128],
                                   cqT[:, cc, g_ * 512:(g_ + 1) * 512],
                                   start=(cc == 0), stop=(cc == 15))
                ins.then_inc(s.sem("tc"), 1)
                s.bump("tc")
        block.tensor(f_tens)

        def f_vec(v):
            v.wait_ge(s.sem("sk"), base["sk"])
            v.wait_ge(s.sem("so"), base["so"])
            for n in range(64):
                g_, qc = divmod(n, 16)
                v.wait_ge(s.sem("ta"), base["ta"] + n + 1)
                v.tensor_copy(cqT[:, qc, g_ * 512:(g_ + 1) * 512], PS[n % 4])
                v.drain().then_inc(s.sem("va"), 1)
                s.bump("va")
            for n in range(64):
                par = n % 2
                v.wait_ge(s.sem("tc"), base["tc"] + n + 1)
                if n >= 2:
                    v.wait_ge(s.sem("sq" if par == 0 else "sq1"),
                              base["sq" if par == 0 else "sq1"] + 16 * (n // 2))
                v.tensor_copy(qev[:, par], PS[n % 4])
                v.drain().then_inc(s.sem("vb"), 1)
                s.bump("vb")
        block.vector(f_vec)

        def f_sync(sy):
            for n in range(64):
                g_, qc = divmod(n, 16)
                par = n % 2
                sy.wait_ge(s.sem("vb"), base["vb"] + n + 1)
                sy.dma_start(out=s.qT_spill[qc, :, g_ * 512:(g_ + 1) * 512],
                             in_=qev[:, par]
                             ).then_inc(s.sem("sq" if par == 0 else "sq1"), 16)
                s.bump("sq" if par == 0 else "sq1", 16)
        block.sync(f_sync)

    def phase_thresh(self, block):
        s = self
        nc = s.nc
        sbat = lambda n, off, shape, dt: nc.alloc_sbuf_tensor_at(n, shape, dt, offset=off).ap()
        lg = sbat("t_lg", SLAB2, [128, 128, 16], F32)
        mske = sbat("t_mske", SLAB2 + 8 * KB, [128, 128, 16], F32)
        mskd = sbat("t_mskd", SLAB2 + 16 * KB, [128, 128, 16], F32)
        msk2 = sbat("t_msk2", SLAB2 + 24 * KB, [128, 128, 16], F32)
        rw8 = sbat("t_rw8", SLAB2 + 32 * KB, [128, 128, 8], F32)
        cmp = sbat("t_cmp", SLAB2 + 36 * KB, [128, 128, 8], F32)
        cntR = sbat("t_cntR", SLAB2 + 40 * KB, [128, 128, 8], F32)
        cnt2 = sbat("t_cnt2", SLAB2 + 44 * KB, [128, 8], F32)
        mm1 = sbat("t_m1", SLAB2 + 45 * KB, [128, 128], F32)
        mm2 = sbat("t_m2", SLAB2 + 46 * KB, [128, 128], F32)
        ww1 = sbat("t_w1", SLAB2 + 47 * KB, [128, 128], F32)
        ww2 = sbat("t_w2", SLAB2 + 48 * KB, [128, 128], F32)
        st8 = sbat("t_st8", SLAB2 + 49 * KB, [128, 8, 8], F32)
        ones128 = sbat("t_ones", SLAB2 + 50 * KB, [128, 128], F32)
        snap = sbat("t_snap", SLAB2 + 51 * KB, [128, 96], F32)
        base = dict(s.cnt)

        def f_gp(g):
            g.wait_ge(s.sem("cc"), base["cc"])
            g.wait_ge(s.sem("tc"), base["tc"])
            g.dma_start(out=lg[:],
                        in_=s.lg_all.rearrange("(p t) e -> p t e", p=128)
                        ).then_inc(s.sem("so"), 16)
            s.bump("so", 16)
        block.gpsimd(f_gp)


        def f_act(sc):
            sc.wait_ge(s.sem("vb"), base["vb"] + 1)
            sc.activation(out=ww1[:], in_=mm1[:], func=AF.Sigmoid)
            sc.drain().then_inc(s.sem("ab"), 1)
            s.bump("ab")
        block.scalar(f_act)

        def f_tens(t):
            for it in range(BIS_IT):
                t.wait_ge(s.sem("vf"), base["vf"] + it + 1)
                ins = t.matmul(s.pp0[:, 0:8], ones128[:], cnt2[:],
                               start=True, stop=True)
                ins.then_inc(s.sem("tb"), 1)
                s.bump("tb")
        block.tensor(f_tens)

        def f_vec(v):
            v.wait_ge(s.sem("so"), base["so"] + 16)
            v.reduce_max(mm1[:], lg[:], axis=AX.X)
            v.drain()
            v.tensor_tensor(mske[:], lg[:],
                            mm1[:, :, None].broadcast_to((128, 128, 16)), ALU.is_equal)
            v.drain()
            v.scalar_tensor_tensor(mskd[:], mske[:], -1e38, lg[:], ALU.mult, ALU.add)
            v.drain()
            v.reduce_max(mm2[:], mskd[:], axis=AX.X)
            v.drain()
            v.tensor_tensor(msk2[:], mskd[:],
                            mm2[:, :, None].broadcast_to((128, 128, 16)), ALU.is_equal)
            v.drain()
            v.tensor_tensor(mm1[:], mm1[:], mm2[:], ALU.subtract)
            v.drain().then_inc(s.sem("vb"), 1)
            s.bump("vb")
            v.wait_ge(s.sem("ab"), base["ab"] + 1)
            v.tensor_scalar(ww2[:], ww1[:], -1.0, 1.0, ALU.mult, ALU.add)
            v.drain()
            v.tensor_tensor(rw8[:], mske[:, :, 8:16],
                            ww1[:, :, None].broadcast_to((128, 128, 8)), ALU.mult)
            v.drain()
            v.tensor_tensor(cmp[:], msk2[:, :, 8:16],
                            ww2[:, :, None].broadcast_to((128, 128, 8)), ALU.mult)
            v.drain()
            v.tensor_tensor(rw8[:], rw8[:], cmp[:], ALU.add)
            v.drain()
            v.memset(st8[:, 0, :], 0.0)
            v.memset(st8[:, 2, :], 1.0)
            v.memset(ones128[:], 1.0)
            v.drain()
            for it in range(BIS_IT):
                par = it % 2
                lo, hi = st8[:, 0 + par, :], st8[:, 2 + par, :]
                lo_n, hi_n = st8[:, 1 - par, :], st8[:, 3 - par, :]
                mid, ge, c8 = st8[:, 4, :], st8[:, 5, :], st8[:, 6, :]
                v.tensor_tensor(mid, lo, hi, ALU.add)
                v.drain()
                v.tensor_scalar_mul(mid, mid, 0.5)
                v.drain()
                v.tensor_tensor(cmp[:], rw8[:],
                                mid[:, None, :].broadcast_to((128, 128, 8)), ALU.is_gt)
                v.drain()
                v.reduce_sum(cnt2[:], cmp[:].rearrange("p t e -> p e t"), axis=AX.X)
                v.drain().then_inc(s.sem("vf"), 1)
                s.bump("vf")
                v.wait_ge(s.sem("tb"), base["tb"] + it + 1)
                v.tensor_single_scalar(ge, s.pp0[:, 0:8], float(CAP), ALU.is_ge)
                v.drain()
                if s.debug and it in (0, 1, 2):
                    v.tensor_copy(snap[:, it * 24: it * 24 + 8], s.pp0[:, 0:8])
                    v.drain()
                    v.tensor_copy(snap[:, it * 24 + 8: it * 24 + 16], mid)
                    v.drain()
                    v.tensor_copy(snap[:, it * 24 + 16: it * 24 + 24], ge)
                    v.drain()
                tmp = st8[:, 7, :]
                v.tensor_tensor(tmp, mid, lo, ALU.subtract)
                v.drain()
                v.tensor_tensor(tmp, tmp, ge, ALU.mult)
                v.drain()
                v.tensor_tensor(lo_n, lo, tmp, ALU.add)
                v.drain()
                v.tensor_tensor(tmp, hi, mid, ALU.subtract)
                v.drain()
                v.tensor_tensor(tmp, tmp, ge, ALU.mult)
                v.drain()
                v.tensor_tensor(hi_n, mid, tmp, ALU.add)
                v.drain()
            lo_fin = st8[:, 0 + BIS_IT % 2, :]
            v.tensor_copy(s.wgt[:, :, 0:8], s.rw[:, :, 0:8])
            v.drain()
            v.tensor_tensor(cmp[:, 0:16, :], s.rw[:, :, 8:16],
                            lo_fin[:, None, :].broadcast_to((128, 16, 8)), ALU.is_gt)
            v.drain()
            v.tensor_tensor(s.wgt[:, :, 8:16], s.rw[:, :, 8:16], cmp[:, 0:16, :],
                            ALU.mult)
            v.drain().then_inc(s.sem("ve"), 1)
            s.bump("ve")
        block.vector(f_vec)

        if s.debug:
            def f_gp2(g):
                g.wait_ge(s.sem("ve"), s.cnt["ve"])
                g.dma_start(out=s.bis_dbg[:, :], in_=snap[:]
                            ).then_inc(s.sem("so"), 16)
                s.bump("so", 16)
                g.dma_start(out=s.thr_dbg[:, :], in_=st8[:, 0 + BIS_IT % 2, :]
                            ).then_inc(s.sem("so"), 16)
                g.dma_start(out=s.wgt_dbg[:, :],
                            in_=s.wgt[:].rearrange("p a b -> p (a b)")
                            ).then_inc(s.sem("so"), 16)
                s.bump("so", 32)
            block.gpsimd(f_gp2)

    def _attn_items(self):
        items = []
        for i in range(6):
            items.append((0, "full", i, None))
        items.append((0, "diag", (0, 0), 0))
        items.append((0, "diag", (0, 1), 512))
        for i in range(14):
            items.append((1, "full", i, None))
        items.append((1, "diag", (1, 0), 0))
        items.append((1, "diag", (1, 1), 512))
        return items

    def phase_attn(self, block):
        s = self
        nc = s.nc
        sbat = lambda n, off, shape, dt: nc.alloc_sbuf_tensor_at(n, shape, dt, offset=off).ap()
        qh = sbat("at_qh", SLAB1, [128, 16, 1024], BF16)
        ckvi = sbat("at_ckv", SLAB1 + 32 * KB, [128, 2, 4, 512], BF16)
        pT = sbat("at_pt", SLAB1 + 40 * KB, [128, 2, 1024], BF16)
        sums = sbat("at_sums", SLAB1 + 44 * KB, [128, 1024], F32)
        rsr = sbat("at_rsr", SLAB1 + 48 * KB, [128, 2, 1024], F32)
        ctxe = sbat("at_ctxe", SLAB1 + 56 * KB, [128, 2, 1024], BF16)
        wuk = sbat("at_wuk", SLAB2, [128, 4, 2048], BF16)
        wuv = sbat("at_wuv", SLAB2 + 16 * KB, [128, 4, 2048], BF16)
        kT = sbat("at_kt", SLAB2 + 32 * KB, [128, 16, 512], BF16)
        vblk = sbat("at_v", SLAB2 + 48 * KB, [128, 4, 2048], BF16)
        ctx = sbat("at_ctx", SLAB3, [128, 16, 1024], F32)
        base = dict(s.cnt)
        items = self._attn_items()
        UPQ = [s.pp0[:, 0:512], s.pp0[:, 512:1024],
               s.pp0[:, 1024:1536], s.pp0[:, 1536:2048]]
        SS = [s.pp0[:, 0:1024], s.pp0[:, 1024:2048]]
        CTXPS = s.pp1[:, 0:1024]
        XPS = s.pp1[:, 1024:2048]

        NIT = len(items)
        dig = [it for it, (p, knd, _, _) in enumerate(items) if knd == "diag"]

        def f_gp(g):
            for p in range(2):
                if p == 0:
                    g.wait_ge(s.sem("sq"), base["sq"])
                    g.wait_ge(s.sem("sq1"), base["sq1"])
                    g.wait_ge(s.sem("ve"), base["ve"])
                    g.wait_ge(s.sem("so"), base["so"])
                    g.dma_start(out=wuk[:],
                                in_=s.w_ukT.rearrange("(kc p) d -> p kc d", p=128)
                                ).then_inc(s.sem("sw"), 16)
                    s.bump("sw", 16)
                    g.dma_start(out=wuv[:],
                                in_=s.w_uvT.rearrange("(kc p) d -> p kc d", p=128)
                                ).then_inc(s.sem("sw"), 16)
                    s.bump("sw", 16)
                else:
                    g.wait_ge(s.sem("tc"), base["tc"] + 64 * 8)
                g.dma_start(out=qh[:],
                            in_=s.qT_spill[:, :, p * 1024:(p + 1) * 1024]
                            .rearrange("dc p t -> p dc t")
                            ).then_inc(s.sem("sw1"), 16)
                s.bump("sw1", 16)
        block.gpsimd(f_gp)

        def f_sync(sy):
            for it, (p, knd, src, rel) in enumerate(items):
                par = it % 2
                if it < 2:
                    sy.wait_ge(s.sem("ve"), base["ve"])
                if it >= 2:
                    sy.wait_ge(s.sem("td"), base["td"] + 32 * (it - 1))
                if knd == "full":
                    srcap = s.ckv_spill[src]
                else:
                    srcap = s.ckv_own[src[0], src[1]]
                sy.dma_start(out=ckvi[:, par],
                             in_=srcap.rearrange("kc p t -> p kc t")
                             ).then_inc(s.sem("sy" if par == 0 else "sy1"), 16)
                s.bump("sy" if par == 0 else "sy1", 16)
        block.sync(f_sync)

        def f_tens(t):
            t.wait_ge(s.sem("sw"), base["sw"] + 32)
            for it, (p, knd, src, rel) in enumerate(items):
                par = it % 2
                first_of_pass = (it == 0 or it == 8)
                t.wait_ge(s.sem("sy" if par == 0 else "sy1"),
                          base["sy" if par == 0 else "sy1"] + 16 * (it // 2 + 1))
                t.wait_ge(s.sem("sw1"), base["sw1"] + 16 * (p + 1))
                for dc in range(16):
                    q = dc % 4
                    if it > 0 or dc >= 4:
                        pass
                    if it == 0 and dc == 0:
                        t.wait_ge(s.sem("vb"), base["vb"])
                    if dc >= 4:
                        t.wait_ge(s.sem("ad"), base["ad"] + 16 * it + dc - 3)
                    elif it > 0:
                        t.wait_ge(s.sem("ab"), base["ab"] + 64 * it)
                    for kc in range(4):
                        ins = t.matmul(UPQ[q][:, 0:512],
                                       wuk[:, kc, dc * 128:(dc + 1) * 128],
                                       ckvi[:, par, kc, :],
                                       start=(kc == 0), stop=(kc == 3))
                    ins.then_inc(s.sem("td"), 1)
                    s.bump("td")
                for n in range(16):
                    sub, dg = divmod(n, 4)
                    q = n % 4
                    if n >= 4:
                        t.wait_ge(s.sem("va"), base["va"] + 16 * it + n - 3)
                    else:
                        t.wait_ge(s.sem("ad"), base["ad"] + 16 * it + 13 + n)
                    for kc in range(4):
                        ins = t.matmul(UPQ[q][:, 0:512],
                                       ckvi[:, par, kc, sub * 128:(sub + 1) * 128],
                                       wuv[:, kc, dg * 512:(dg + 1) * 512],
                                       start=(kc == 0), stop=(kc == 3))
                    ins.then_inc(s.sem("td"), 1)
                    s.bump("td")
                for n_ in range(65):
                    if n_ < 64:
                        h, kt = divmod(n_, 4)
                        sl = n_ % 2
                        if n_ == 0:
                            t.wait_ge(s.sem("ad"), base["ad"] + 16 * (it + 1))
                            t.wait_ge(s.sem("va"), base["va"] + 16 * (it + 1))
                        prev_user = 64 * it + n_ - 2
                        if prev_user >= 0:
                            t.wait_ge(s.sem("ab"), base["ab"] + prev_user + 1)
                        for half in range(2):
                            ins = t.matmul(SS[sl][:, half * 512:(half + 1) * 512],
                                           kT[:, h, kt * 128:(kt + 1) * 128],
                                           qh[:, h, half * 512:(half + 1) * 512],
                                           start=True, stop=True)
                        ins.then_inc(s.sem("tc"), 1)
                        s.bump("tc")
                    if n_ >= 1:
                        m_ = n_ - 1
                        h, kt = divmod(m_, 4)
                        nn_ = 64 * it + m_
                        if knd == "diag":
                            t.wait_ge(s.sem("vd"), base["vd"] + dig.index(it) * 64
                                      + m_ + 1)
                        else:
                            t.wait_ge(s.sem("ab"), base["ab"] + nn_ + 1)
                        if kt == 0:
                            hh = 16 * it + h
                            if hh > 0:
                                t.wait_ge(s.sem("vc"), base["vc"] + hh)
                        for half in range(2):
                            ins = t.matmul(CTXPS[:, half * 512:(half + 1) * 512],
                                           vblk[:, kt, h * 128:(h + 1) * 128],
                                           pT[:, nn_ % 2, half * 512:(half + 1) * 512],
                                           start=(kt == 0), stop=(kt == 3))
                        if m_ == 0 and it > 0:
                            t.wait_ge(s.sem("ve"), base["ve"] + it)
                        for half in range(2):
                            ins = t.matmul(XPS[:, half * 512:(half + 1) * 512],
                                           s.eh[:, h, :],
                                           pT[:, nn_ % 2, half * 512:(half + 1) * 512],
                                           start=(m_ == 0),
                                           stop=(m_ == 63))
                        ins.then_inc(s.sem("te"), 1)
                        s.bump("te")
        block.tensor(f_tens)

        def f_act(sc):
            for it, (p, knd, src, rel) in enumerate(items):
                for dc in range(16):
                    q = dc % 4
                    sc.wait_ge(s.sem("td"), base["td"] + 32 * it + dc + 1)
                    sc.copy(kT[:, dc, :], UPQ[q][:, 0:512])
                    sc.drain().then_inc(s.sem("ad"), 1)
                    s.bump("ad")
                for h in range(NH):
                    for kt in range(4):
                        nn_ = 64 * it + h * 4 + kt
                        sl = (h * 4 + kt) % 2
                        sc.wait_ge(s.sem("tc"), base["tc"] + nn_ + 1)
                        if nn_ >= 2:
                            sc.wait_ge(s.sem("te"), base["te"] + nn_ - 1)
                        sc.activation(out=pT[:, nn_ % 2, :], in_=SS[sl][:, :],
                                      func=AF.Exp, scale=SCALE,
                                      bias=s.biass[:, it:it + 1])
                        sc.drain().then_inc(s.sem("ab"), 1)
                        s.bump("ab")
        block.scalar(f_act)

        def f_vec(v):
            for it, (p, knd, src, rel) in enumerate(items):
                par = it % 2
                first_of_pass = (it == 0 or it == 8)
                for n in range(16):
                    sub, dg = divmod(n, 4)
                    q = n % 4
                    v.wait_ge(s.sem("td"), base["td"] + 32 * it + 16 + n + 1)
                    v.tensor_copy(vblk[:, sub, dg * 512:(dg + 1) * 512],
                                  UPQ[q][:, 0:512])
                    v.drain().then_inc(s.sem("va"), 1)
                    s.bump("va")
                for h in range(NH):
                    if knd == "diag":
                        for kt in range(4):
                            nn_ = 64 * it + h * 4 + kt
                            inv = rel + kt * 128
                            v.wait_ge(s.sem("ab"), base["ab"] + nn_ + 1)
                            if inv > 0:
                                v.memset(pT[:, nn_ % 2, 0:inv], 0.0)
                                v.drain()
                            v.tensor_tensor(pT[:, nn_ % 2, inv:inv + 128],
                                            pT[:, nn_ % 2, inv:inv + 128],
                                            s.tril[:], ALU.mult)
                            v.drain().then_inc(s.sem("vd"), 1)
                            s.bump("vd")
                    v.wait_ge(s.sem("te"), base["te"] + 64 * it + 4 * h + 4)
                    dst = ctx[:, h, :]
                    if first_of_pass:
                        v.tensor_copy(dst, CTXPS[:, :])
                    else:
                        v.tensor_tensor(dst, dst, CTXPS[:, :], ALU.add)
                    v.drain().then_inc(s.sem("vc"), 1)
                    s.bump("vc")
                v.wait_ge(s.sem("te"), base["te"] + 64 * (it + 1))
                if first_of_pass:
                    v.tensor_copy(sums[:], XPS[:, :])
                else:
                    v.tensor_tensor(sums[:], sums[:], XPS[:, :], ALU.add)
                v.drain().then_inc(s.sem("ve"), 1)
                s.bump("ve")
                if it == 7 or it == 23:
                    v.reciprocal(sums[:], sums[:])
                    v.drain().then_inc(s.sem("vf"), 1)
                    s.bump("vf")
                    for h in range(NH):
                        v.wait_ge(s.sem("so"),
                                  base["so"] + 16 * (33 * p + 2 * h + 2))
                        v.tensor_tensor(ctxe[:, h % 2, :], ctx[:, h, :],
                                        rsr[:, h % 2, :], ALU.mult)
                        v.drain().then_inc(s.sem("vb"), 1)
                        s.bump("vb")
        block.vector(f_vec)

        def f_gp2(g):
            for p in range(2):
                g.wait_ge(s.sem("vf"), base["vf"] + p + 1)
                g.dma_start(out=s.rs_dram[p], in_=sums[0:16, :]
                            ).then_inc(s.sem("so"), 16)
                s.bump("so", 16)
                for h in range(NH):
                    g.wait_ge(s.sem("so"), s.cnt["so"])
                    if h >= 2:
                        g.wait_ge(s.sem("vb"), base["vb"] + 16 * p + h - 1)
                    g.dma_start(out=rsr[:, h % 2, :],
                                in_=s.rs_dram[p, h, :].partition_broadcast(128)
                                ).then_inc(s.sem("so"), 16)
                    s.bump("so", 16)
                    g.wait_ge(s.sem("vb"), base["vb"] + 16 * p + h + 1)
                    g.dma_start(out=s.ctx_spill[p, h],
                                in_=ctxe[:, h % 2, :]).then_inc(s.sem("so"), 16)
                    s.bump("so", 16)
        block.gpsimd(f_gp2)

    def phase_ffn1(self, block):
        s = self
        nc = s.nc
        sbat = lambda n, off, shape, dt: nc.alloc_sbuf_tensor_at(n, shape, dt, offset=off).ap()
        xnT = sbat("f1_xnT", SLAB1, [128, 16, 2048], BF16)
        ug = sbat("f1_ug", SLAB2, [128, 2, 2, 16, 512], BF16)
        ire = sbat("f1_ir", SLAB3, [128, 2, 4, 2048], BF16)
        sil = sbat("f1_sil", SLAB3 + 32 * KB, [128, 2, 512], F32)
        base = dict(s.cnt)

        def f_gp(g):
            g.wait_ge(s.sem("so"), base["so"])
            g.wait_ge(s.sem("tc"), base["tc"])
            g.wait_ge(s.sem("te"), base["te"])
            g.wait_ge(s.sem("vb"), base["vb"])
            g.dma_start(out=xnT[:],
                        in_=s.xn_spill.rearrange("cc p t -> p cc t")
                        ).then_inc(s.sem("so"), 16)
            s.bump("so", 16)
            for e in range(NE + 1):
                if e < NE:
                    par = e % 2
                    if e >= 2:
                        g.wait_ge(s.sem("tc"), base["tc"] + 32 * (e - 1))
                    g.dma_start(out=ug[:, par, 0],
                                in_=s.uT[:, e * EW:(e + 1) * EW]
                                .rearrange("(cc p) w -> p cc w", p=128)
                                ).then_inc(s.sem("sw" if par == 0 else "sw1"), 16)
                    g.dma_start(out=ug[:, par, 1],
                                in_=s.gT[:, e * EW:(e + 1) * EW]
                                .rearrange("(cc p) w -> p cc w", p=128)
                                ).then_inc(s.sem("sw" if par == 0 else "sw1"), 16)
                    s.bump("sw" if par == 0 else "sw1", 32)
                if e >= 1:
                    ep = e - 1
                    parp = ep % 2
                    g.wait_ge(s.sem("va"), base["va"] + 16 * (ep + 1))
                    g.dma_start(out=s.ir_spill[ep].rearrange("wc p t -> p wc t"),
                                in_=ire[:, parp]
                                ).then_inc(s.sem("sk" if parp == 0 else "sk1"), 16)
                    s.bump("sk" if parp == 0 else "sk1", 16)
        block.gpsimd(f_gp)

        def f_tens(t):
            t.wait_ge(s.sem("so"), base["so"] + 16)
            for e in range(NE):
                par = e % 2
                t.wait_ge(s.sem("sw" if par == 0 else "sw1"),
                          base["sw" if par == 0 else "sw1"] + 32 * (e // 2 + 1))
                for n in range(16):
                    tg, wc = divmod(n, 4)
                    nn_ = 32 * e + 2 * n
                    slg = s.pp0[:, (n % 2) * 1024:(n % 2) * 1024 + 512]
                    slu = s.pp1[:, (n % 2) * 1024:(n % 2) * 1024 + 512]
                    if nn_ >= 4:
                        t.wait_ge(s.sem("ab"), base["ab"] + 16 * e + n - 1)
                    for cc in range(16):
                        ins = t.matmul(slg, ug[:, par, 1, cc, wc * 128:(wc + 1) * 128],
                                       xnT[:, cc, tg * 512:(tg + 1) * 512],
                                       start=(cc == 0), stop=(cc == 15))
                    ins.then_inc(s.sem("tc"), 1)
                    s.bump("tc")
                    if nn_ >= 4:
                        t.wait_ge(s.sem("va"), base["va"] + 16 * e + n - 1)
                    for cc in range(16):
                        ins = t.matmul(slu, ug[:, par, 0, cc, wc * 128:(wc + 1) * 128],
                                       xnT[:, cc, tg * 512:(tg + 1) * 512],
                                       start=(cc == 0), stop=(cc == 15))
                    ins.then_inc(s.sem("tc"), 1)
                    s.bump("tc")
        block.tensor(f_tens)

        def f_act(sc):
            for e in range(NE):
                for n in range(16):
                    slg = s.pp0[:, (n % 2) * 1024:(n % 2) * 1024 + 512]
                    sc.wait_ge(s.sem("tc"), base["tc"] + 32 * e + 2 * n + 1)
                    if 16 * e + n >= 2:
                        sc.wait_ge(s.sem("va"), base["va"] + 16 * e + n - 1)
                    sc.activation(out=sil[:, n % 2, :], in_=slg, func=AF.Silu)
                    sc.drain().then_inc(s.sem("ab"), 1)
                    s.bump("ab")
        block.scalar(f_act)

        def f_vec(v):
            for e in range(NE):
                par = e % 2
                for n in range(16):
                    tg, wc = divmod(n, 4)
                    slu = s.pp1[:, (n % 2) * 1024:(n % 2) * 1024 + 512]
                    v.wait_ge(s.sem("tc"), base["tc"] + 32 * e + 2 * n + 2)
                    v.wait_ge(s.sem("ab"), base["ab"] + 16 * e + n + 1)
                    if n == 0 and e >= 2:
                        ps = "sk" if par == 0 else "sk1"
                        v.wait_ge(s.sem(ps), base[ps] + 16 * (e // 2))
                    v.tensor_tensor(ire[:, par, wc, tg * 512:(tg + 1) * 512],
                                    slu, sil[:, n % 2, :], ALU.mult)
                    v.drain().then_inc(s.sem("va"), 1)
                    s.bump("va")
        block.vector(f_vec)

    def phase_ffn2(self, block):
        s = self
        nc = s.nc
        sbat = lambda n, off, shape, dt: nc.alloc_sbuf_tensor_at(n, shape, dt, offset=off).ap()
        acc0 = sbat("f2_acc0", SLAB1, [128, 8, 2048], F32)
        acc1 = sbat("f2_acc1", SLAB3, [128, 8, 2048], F32)
        dte = sbat("f2_dt", SLAB2, [128, 2, 4, 2048], BF16)
        iri = sbat("f2_ir", SLAB2 + 32 * KB, [128, 2, 4, 2048], BF16)
        base = dict(s.cnt)
        self.F2_bufs = dict(acc0=acc0, acc1=acc1)

        def f_gp(g):
            g.wait_ge(s.sem("tc"), base["tc"])
            for e in range(NE):
                par = e % 2
                if e >= 2:
                    g.wait_ge(s.sem("tc"), base["tc"] + 16 * (e - 1))
                g.dma_start(out=dte[:, par],
                            in_=s.dT[e * EW:(e + 1) * EW, :]
                            .rearrange("(wc p) h -> p wc h", p=128)
                            ).then_inc(s.sem("sw" if par == 0 else "sw1"), 16)
                s.bump("sw" if par == 0 else "sw1", 16)
        block.gpsimd(f_gp)

        def f_sync(sy):
            sy.wait_ge(s.sem("tc"), base["tc"])
            sy.wait_ge(s.sem("sk"), base["sk"])
            sy.wait_ge(s.sem("sk1"), base["sk1"])
            for e in range(NE):
                par = e % 2
                if e >= 2:
                    sy.wait_ge(s.sem("tc"), base["tc"] + 16 * (e - 1))
                sy.dma_start(out=iri[:, par],
                             in_=s.ir_spill[e].rearrange("wc p t -> p wc t")
                             ).then_inc(s.sem("sy" if par == 0 else "sy1"), 16)
                s.bump("sy" if par == 0 else "sy1", 16)
        block.sync(f_sync)

        def f_tens(t):
            for e in range(NE):
                par = e % 2
                wsem = "sw" if par == 0 else "sw1"
                ksem = "sy" if par == 0 else "sy1"
                t.wait_ge(s.sem(wsem), base[wsem] + 16 * (e // 2 + 1))
                t.wait_ge(s.sem(ksem), base[ksem] + 16 * (e // 2 + 1))
                for ti in range(16):
                    n = 16 * e + ti
                    ps = s.pp0 if n % 2 == 0 else s.pp1
                    if n >= 2:
                        t.wait_ge(s.sem("va"), base["va"] + n - 1)
                    for hg in range(4):
                        for wc in range(4):
                            ins = t.matmul(ps[:, hg * 512:(hg + 1) * 512],
                                           iri[:, par, wc, ti * 128:(ti + 1) * 128],
                                           dte[:, par, wc, hg * 512:(hg + 1) * 512],
                                           start=(wc == 0), stop=(wc == 3))
                    ins.then_inc(s.sem("tc"), 1)
                    s.bump("tc")
        block.tensor(f_tens)

        def f_vec(v):
            for e in range(NE):
                for ti in range(16):
                    n = 16 * e + ti
                    ps = s.pp0 if n % 2 == 0 else s.pp1
                    acc = acc0 if ti < 8 else acc1
                    sl = acc[:, ti % 8, :]
                    wcol = s.wgt[:, ti, e:e + 1]
                    v.wait_ge(s.sem("tc"), base["tc"] + n + 1)
                    if e == 0:
                        v.tensor_scalar(sl, ps[:, :], wcol, None, ALU.mult)
                    else:
                        v.scalar_tensor_tensor(sl, ps[:, :], wcol, sl,
                                               ALU.mult, ALU.add)
                    v.drain().then_inc(s.sem("va"), 1)
                    s.bump("va")
        block.vector(f_vec)

    def phase_final(self, block):
        s = self
        nc = s.nc
        sbat = lambda n, off, shape, dt: nc.alloc_sbuf_tensor_at(n, shape, dt, offset=off).ap()
        wo = sbat("fn_wo", SLAB2, [128, 16, 1024], BF16)
        ctxt = sbat("fn_ctx", SLAB2 + 32 * KB, [128, 2, 16, 128], BF16)
        xfin = sbat("fn_x", SLAB2 + 40 * KB, [128, 2, 1024], F32)
        outt = sbat("fn_out", SLAB2 + 48 * KB, [128, 2, 1024], F32)
        outt2 = sbat("fn_out2", SLAB2 + 56 * KB, [128, 2, 1024], F32)
        bo = sbat("fn_bo", OFF_EH, [1, 2048], BF16)
        acc0 = self.F2_bufs["acc0"]
        acc1 = self.F2_bufs["acc1"]
        base = dict(s.cnt)

        def f_gp(g):
            g.wait_ge(s.sem("te"), base["te"])
            g.wait_ge(s.sem("tc"), base["tc"])
            g.dma_start(out=bo[:], in_=s.b_o[:, :]).then_inc(s.sem("sw"), 16)
            s.bump("sw", 16)
            for ch in range(2):
                if ch == 1:
                    g.wait_ge(s.sem("tc"), base["tc"] + 16)
                g.dma_start(out=wo[:],
                            in_=s.w_oT.rearrange("(dc p) c -> p dc c", p=128)
                            [:, :, ch * 1024:(ch + 1) * 1024]
                            ).then_inc(s.sem("sw1"), 16)
                s.bump("sw1", 16)
        block.gpsimd(f_gp)

        def f_act(sc):
            sc.wait_ge(s.sem("va"), base["va"])
            for ti in range(16):
                acc = acc0 if ti < 8 else acc1
                sc.activation(out=acc[:, ti % 8, :], in_=acc[:, ti % 8, :],
                              func=AF.Silu)
                sc.drain().then_inc(s.sem("ab"), 1)
                s.bump("ab")
        block.scalar(f_act)

        def emit_out_dma(sy, m):
            mpar = m % 2
            ch_, ti_ = divmod(m, 16)
            sy.wait_ge(s.sem("vb"), base["vb"] + m + 1)
            sy.dma_start(out=s.out[ti_ * 128:(ti_ + 1) * 128,
                                   ch_ * 1024:(ch_ + 1) * 1024],
                         in_=outt2[:, mpar, :]
                         ).then_inc(s.sem("sy" if mpar == 0 else "sy1"), 16)
            s.bump("sy" if mpar == 0 else "sy1", 16)

        def f_sync(sy):
            sy.wait_ge(s.sem("tc"), base["tc"])
            for ch in range(2):
                for ti in range(16):
                    n = ch * 16 + ti
                    par = n % 2
                    if n >= 2:
                        emit_out_dma(sy, n - 2)
                        sy.wait_ge(s.sem("tc"), base["tc"] + n - 1)
                    p = ti // 8
                    sy.dma_start(out=ctxt[:, par],
                                 in_=s.ctx_spill[p, :, :, (ti % 8) * 128:
                                                 (ti % 8) * 128 + 128]
                                 .rearrange("dc p t -> p dc t")
                                 ).then_inc(s.sem("sq" if par == 0 else "sq1"), 16)
                    s.bump("sq" if par == 0 else "sq1", 16)
                    if n >= 2:
                        sy.wait_ge(s.sem("vb"), base["vb"] + n - 1)
                    sy.dma_start(out=xfin[:, par, :],
                                 in_=s.x_own[ti * 128:(ti + 1) * 128,
                                             ch * 1024:(ch + 1) * 1024]
                                 ).then_inc(s.sem("sx" if par == 0 else "sx1"), 16)
                    s.bump("sx" if par == 0 else "sx1", 16)
            emit_out_dma(sy, 30)
            emit_out_dma(sy, 31)
        block.sync(f_sync)

        def f_tens(t):
            t.wait_ge(s.sem("sw"), base["sw"] + 16)
            t.wait_ge(s.sem("va"), base["va"])
            for ch in range(2):
                t.wait_ge(s.sem("sw1"), base["sw1"] + 16 * (ch + 1))
                for ti in range(16):
                    n = ch * 16 + ti
                    par = n % 2
                    ps = (s.pp0 if par == 0 else s.pp1)[:, 0:1024]
                    t.wait_ge(s.sem("sq" if par == 0 else "sq1"),
                              base["sq" if par == 0 else "sq1"] + 16 * (n // 2 + 1))
                    if n >= 2:
                        t.wait_ge(s.sem("vb"), base["vb"] + n - 1)
                    for dc in range(16):
                        for half in range(2):
                            t.matmul(ps[:, half * 512:(half + 1) * 512],
                                     ctxt[:, par, dc, :],
                                     wo[:, dc, half * 512:(half + 1) * 512],
                                     start=(dc == 0), stop=False)
                    for half in range(2):
                        ins = t.matmul(ps[:, half * 512:(half + 1) * 512],
                                       s.ones_row[:],
                                       bo[0:1, ch * 1024 + half * 512:
                                          ch * 1024 + (half + 1) * 512],
                                       start=False, stop=True)
                    ins.then_inc(s.sem("tc"), 1)
                    s.bump("tc")
        block.tensor(f_tens)

        def f_vec(v):
            for ch in range(2):
                for ti in range(16):
                    n = ch * 16 + ti
                    par = n % 2
                    ps = (s.pp0 if par == 0 else s.pp1)[:, 0:1024]
                    acc = acc0 if ti < 8 else acc1
                    gsl = acc[:, ti % 8, ch * 1024:(ch + 1) * 1024]
                    v.wait_ge(s.sem("tc"), base["tc"] + n + 1)
                    v.wait_ge(s.sem("ab"), base["ab"] + ti + 1)
                    v.wait_ge(s.sem("sx" if par == 0 else "sx1"),
                              base["sx" if par == 0 else "sx1"]
                              + 16 * (n // 2 + 1))
                    if n >= 2:
                        ksem = "sy" if par == 0 else "sy1"
                        v.wait_ge(s.sem(ksem), base[ksem] + 16 * (n // 2))
                    v.scalar_tensor_tensor(outt[:, par, :], ps, 1.0, gsl,
                                           ALU.mult, ALU.mult)
                    v.drain()
                    v.scalar_tensor_tensor(outt2[:, par, :], outt[:, par, :],
                                           1.0, xfin[:, par, :],
                                           ALU.mult, ALU.add)
                    v.drain().then_inc(s.sem("vb"), 1)
                    s.bump("vb")
        block.vector(f_vec)

        def f_gp2(g):
            g.wait_ge(s.sem("sy"), s.cnt["sy"])
            g.wait_ge(s.sem("sy1"), s.cnt["sy1"])
        block.gpsimd(f_gp2)

    def build(self, upto="final"):
        nc = self.nc
        order = ["prepare", "A", "B", "thresh", "attn", "ffn1", "ffn2", "final"]
        fns = dict(prepare=self.phase_prepare, A=self.phase_A, B=self.phase_B,
                   thresh=self.phase_thresh, attn=self.phase_attn,
                   ffn1=self.phase_ffn1, ffn2=self.phase_ffn2,
                   final=self.phase_final)
        with nc.Block() as block:
            for name in order:
                fns[name](block)
                if name == upto:
                    break
            if upto != "final":
                self.phase_epilogue(block)
        return nc

    def phase_epilogue(self, block):
        s = self

        def f_gp(g):
            g.wait_ge(s.sem("cc"), s.cnt["cc"])
            g.wait_ge(s.sem("so"), s.cnt["so"])
            g.wait_ge(s.sem("sk"), s.cnt["sk"])
            g.wait_ge(s.sem("ve"), s.cnt["ve"])
            if s.debug:
                g.dma_start(out=s.log_dbg[:, :],
                            in_=s.log[:].rearrange("p a b -> p (a b)")
                            ).then_inc(s.sem("so"), 16)
                g.dma_start(out=s.rw_dbg[:, :],
                            in_=s.rw[:].rearrange("p a b -> p (a b)")
                            ).then_inc(s.sem("so"), 16)
                s.bump("so", 32)
            g.dma_start(out=s.out[:, :], in_=s.x_own[:, :]).then_inc(s.sem("so"), 16)
            s.bump("so", 16)
            g.wait_ge(s.sem("so"), s.cnt["so"])
        block.gpsimd(f_gp)


def _stage_inputs(inputs):
    x = np.asarray(inputs["L0_attn_x"])
    T = lambda a: np.ascontiguousarray(np.asarray(a).T)
    w = {
        "w_dkvT": T(inputs["L0_attn_W_DKV"]),
        "w_dqT": T(inputs["L0_attn_W_DQ"]),
        "w_uqT": T(inputs["L0_attn_W_UQ"]),
        "w_ukT": np.ascontiguousarray(np.asarray(inputs["L0_attn_W_UK"]).T),
        "w_uvT": np.ascontiguousarray(np.asarray(inputs["L0_attn_W_UV"]).T),
        "w_oT": T(inputs["L0_attn_W_O"]),
        "b_o": np.asarray(inputs["L0_attn_b_O"]).reshape(1, D),
        "gate_wT": T(inputs["L0_ffn_gate_w"]),
        "uT": T(inputs["L0_ffn_u"]),
        "gT": T(inputs["L0_ffn_g"]),
        "dT": T(inputs["L0_ffn_d"]),
    }
    tril = np.triu(np.ones((128, 128), dtype=np.float32))
    ident = np.eye(128, dtype=np.float32)
    in_maps = []
    for core in range(8):
        b, j = divmod(core, 4)
        xs = np.ascontiguousarray(x[b])
        c1 = x[b, j * 1024:(j + 1) * 1024]
        c2 = x[b, (7 - j) * 1024:(8 - j) * 1024]
        xo = np.ascontiguousarray(np.concatenate([c1, c2], axis=0))
        bias = np.zeros((128, 32), dtype=np.float32)
        for it in range(6):
            if it >= 2 * j:
                bias[:, it] = NEGB
        for it in range(14):
            if it >= 2 * (7 - j):
                bias[:, 8 + it] = NEGB
        m = dict(w)
        m.update({
            "x_seq": xs, "x_own": xo,
            "tril_in": tril, "bias_in": bias, "ident_in": ident,
        })
        in_maps.append(m)
    return in_maps


def kernel(**inputs):
    in_maps = _stage_inputs(inputs)
    mk = MK(debug=False)
    nc = mk.build()
    res = run_bass_kernel_spmd(nc, in_maps, core_ids=list(range(8)))
    x = np.asarray(inputs["L0_attn_x"])
    out = np.empty((B, SEQ, D), dtype=np.float32)
    for core in range(8):
        b, j = divmod(core, 4)
        o = res.results[core]["out"]
        out[b, j * 1024:(j + 1) * 1024] = o[0:1024]
        out[b, (7 - j) * 1024:(8 - j) * 1024] = o[1024:2048]
    return out


# version 15
# speedup vs baseline: 1.0238x; 1.0057x over previous
import math
import numpy as np
import ml_dtypes
import concourse.bass as bass
import concourse.mybir as mybir
from concourse.bass_utils import run_bass_kernel_spmd

F32 = mybir.dt.float32
BF16 = mybir.dt.bfloat16
AF = mybir.ActivationFunctionType
ALU = mybir.AluOpType
AX = mybir.AxisListType

B, SEQ, D = 2, 8192, 2048
DKV, NH, DH = 512, 16, 128
NE, EW = 16, 512
NTOK = 2048
NT_OWN = 16
NT_SEQ = 64
NT_ALL = NT_SEQ + NT_OWN
SCALE = 1.0 / math.sqrt(DH)
EPS = 1e-5
NEGB = -1e30
CAP = 1024
BIS_IT = 24

KB = 1024
SBB = 16512
OFF_TRIL = SBB + 0
OFF_BIAS = SBB + 512
OFF_EH = SBB + 640
OFF_RW = SBB + 4736
OFF_WGT = SBB + 5760
OFF_LOG = SBB + 6784
OFF_SMALL = SBB + 7808
OFF_ONES = SBB + 8448
OFF_IDF = SBB + 8704
OFF_IDB = SBB + 9216
SLAB1 = SBB + 12288
SLAB2 = SLAB1 + 64 * KB
SLAB3 = SLAB2 + 64 * KB
SLAB_END = SLAB3 + 64 * KB
assert SLAB_END <= 229344


class MK:
    def __init__(self, debug=False, coll=True):
        self.debug = debug
        self.coll = coll
        nc = bass.Bass("TRN2", target_bir_lowering=False)
        self.nc = nc
        dbg = "ExternalOutput" if debug else "Internal"

        di = lambda n, s: nc.dram_tensor(n, s, F32, kind="ExternalInput").ap()
        self.x_seq = di("x_seq", [SEQ, D])
        self.x_own = di("x_own", [NTOK, D])
        self.w_dkvT = di("w_dkvT", [D, DKV])
        self.w_dqT = di("w_dqT", [D, D])
        self.w_uqT = di("w_uqT", [D, D])
        self.w_ukT = di("w_ukT", [DKV, D])
        self.w_uvT = di("w_uvT", [DKV, D])
        self.w_oT = di("w_oT", [D, D])
        self.b_o = di("b_o", [1, D])
        self.gate_wT = di("gate_wT", [D, NE])
        self.uT = di("uT", [D, NE * EW])
        self.gT = di("gT", [D, NE * EW])
        self.dT = di("dT", [NE * EW, D])
        self.tril_in = di("tril_in", [128, 128])
        self.bias_in = di("bias_in", [128, 32])
        self.ident_in = di("ident_in", [128, 128])
        self.out = nc.dram_tensor("out", [NTOK, D], F32, kind="ExternalOutput").ap()

        dr = lambda n, s, dt, kind="Internal": nc.dram_tensor(n, s, dt, kind=kind).ap()
        self.ckv_spill = dr("ckv_spill", [16, 4, 128, 512], BF16, dbg)
        self.ckv_own = dr("ckv_own", [2, 2, 4, 128, 512], BF16, dbg)
        self.xn_spill = dr("xn_spill", [16, 128, NTOK], BF16, dbg)
        self.qT_spill = dr("qT_spill", [16, 128, NTOK], BF16, dbg)
        self.ctx_spill = dr("ctx_spill", [2, 16, 128, 1024], BF16, dbg)
        self.ir_spill = dr("ir_spill", [16, 4, 128, NTOK], BF16, dbg)
        self.rs_dram = dr("rs_dram", [2, NH, 1024], F32, dbg)
        self.cnt_dram = dr("cnt_dram", [128, 8], F32)
        self.lg_local = dr("lg_local", [NTOK, NE], F32)
        self.lg_all = dr("lg_all", [8 * NTOK, NE], F32)
        if debug:
            self.bis_dbg = dr("bis_dbg", [128, 96], F32, "ExternalOutput")
            self.thr_dbg = dr("thr_dbg", [128, 8], F32, "ExternalOutput")
            self.wgt_dbg = dr("wgt_dbg", [128, 256], F32, "ExternalOutput")
            self.log_dbg = dr("log_dbg", [128, 256], F32, "ExternalOutput")
            self.rw_dbg = dr("rw_dbg", [128, 256], F32, "ExternalOutput")

        sb = lambda n, off, shape, dt: nc.alloc_sbuf_tensor_at(n, shape, dt, offset=off).ap()
        self.tril = sb("s_tril", OFF_TRIL, [128, 128], F32)
        self.biass = sb("s_bias", OFF_BIAS, [128, 32], F32)
        self.eh = sb("s_eh", OFF_EH, [128, NH, 128], BF16)
        self.rw = sb("s_rw", OFF_RW, [128, 16, 16], F32)
        self.wgt = sb("s_wgt", OFF_WGT, [128, 16, 16], F32)
        self.log = sb("s_log", OFF_LOG, [128, 16, 16], F32)
        self.small = sb("s_small", OFF_SMALL, [128, 160], F32)
        self.ones_row = sb("s_ones", OFF_ONES, [1, 128], BF16)
        self.idf = sb("s_idf", OFF_IDF, [128, 128], F32)
        self.idb = sb("s_idb", OFF_IDB, [128, 128], BF16)

        self.pp0 = nc.alloc_psum_tensor("pp0", [128, 2048], F32).ap()
        self.pp1 = nc.alloc_psum_tensor("pp1", [128, 2048], F32).ap()

        self.sems = {}
        for name in ["sx", "sx1", "sx2", "sx3",
                     "sw", "sw1", "sk", "sk1", "sq", "sq1", "so",
                     "sy", "sy1",
                     "va", "vb", "vc", "vd", "ve", "vf",
                     "aa", "ab", "ac_", "ad",
                     "ta", "tb", "tc", "td", "te",
                     "cc"]:
            self.sems[name] = nc.alloc_semaphore(name)
        self.cnt = {k: 0 for k in self.sems}

    def sem(self, n):
        return self.sems[n]

    def bump(self, n, k=1):
        self.cnt[n] += k
        return self.cnt[n]

    def sm(self, slot, par=0):
        return self.small[:, slot + par: slot + par + 1]

    def phase_prepare(self, block):
        s = self

        def f_sync(sy):
            sy.dma_start(out=s.tril[:], in_=s.tril_in[:]).then_inc(s.sem("sx"), 16)
            sy.dma_start(out=s.biass[:], in_=s.bias_in[:]).then_inc(s.sem("sx"), 16)
            sy.dma_start(out=s.idf[:], in_=s.ident_in[:]).then_inc(s.sem("sx"), 16)
            s.bump("sx", 48)
        block.sync(f_sync)

        def f_gp(g):
            g.dma_start(out=s.idb[:], in_=s.ident_in[:]).then_inc(s.sem("sw"), 16)
            s.bump("sw", 16)
        block.gpsimd(f_gp)

        def f_vec(v):
            v.memset(s.eh[:], 0.0)
            v.drain()
            for h in range(NH):
                v.memset(s.eh[:, h, h: h + 1], 1.0)
            v.memset(s.ones_row[:], 1.0)
            v.drain().then_inc(s.sem("va"), 1)
            s.bump("va")
        block.vector(f_vec)

    def phase_A(self, block):
        s = self
        nc = s.nc
        NT = NT_ALL

        sbat = lambda n, off, shape, dt: nc.alloc_sbuf_tensor_at(n, shape, dt, offset=off).ap()
        A2 = SLAB2
        wdkv = sbat("a_wdkv", A2, [128, 16, 512], BF16)
        gwt = sbat("a_gwt", A2 + 16 * KB, [128, 16, 16], F32)
        xtile = sbat("a_x", A2 + 17 * KB, [128, 4, D], F32)
        xnf32 = sbat("a_xnf32", A2 + 49 * KB, [128, 2, D], F32)
        A3 = SLAB3
        xnTg = sbat("a_xnTg", A3, [128, 16, 512], BF16)
        xnTf = sbat("a_xnTf", A3 + 16 * KB, [128, 2, 16, 128], F32)
        ckvo = sbat("a_ckvo", A3 + 32 * KB, [128, 4, 512], BF16)
        msk1 = sbat("a_msk1", A3 + 36 * KB, [128, 16, 16], F32)
        msk2 = sbat("a_msk2", A3 + 37 * KB, [128, 16, 16], F32)
        sqs = sbat("a_sq", A3 + 38 * KB, [128, D], F32)
        xnbf = sbat("a_xnbf", A3 + 46 * KB, [128, 4, D], BF16)
        xnT_own = sbat("xnT_own", SLAB1, [128, 16, NTOK], BF16)
        self.wB_early = sbat("b_w", SLAB2, [128, 16, 2048], BF16)
        self.A_bufs = dict(xnT_own=xnT_own)

        def x_src(i):
            if i < NT_SEQ:
                return s.x_seq[i * 128:(i + 1) * 128, :]
            k = i - NT_SEQ
            return s.x_own[k * 128:(k + 1) * 128, :]

        base = dict(s.cnt)

        SX = ["sx", "sx1", "sx2", "sx3"]
        def f_sync(sy):
            for i in range(NT):
                if i >= 4:
                    sy.wait_ge(s.sem("ac_"), base["ac_"] + i - 3)
                sy.dma_start(out=xtile[:, i % 4], in_=x_src(i)
                             ).then_inc(s.sem(SX[i % 4]), 16)
                s.bump(SX[i % 4], 16)
        block.sync(f_sync)

        def f_gp(g):
            g.dma_start(out=wdkv[:],
                        in_=s.w_dkvT.rearrange("(cc p) kv -> p cc kv", p=128)
                        ).then_inc(s.sem("sw"), 16)
            g.dma_start(out=gwt[:],
                        in_=s.gate_wT.rearrange("(cc p) e -> p cc e", p=128)
                        ).then_inc(s.sem("sw"), 16)
            s.bump("sw", 32)
            for gi in range(20):
                g.wait_ge(s.sem("ad"), base["ad"] + 4 * (gi + 1))
                if gi < 16:
                    dst = s.ckv_spill[gi].rearrange("kc p t -> p kc t")
                else:
                    k = gi - 16
                    dst = s.ckv_own[k // 2, k % 2].rearrange("kc p t -> p kc t")
                g.dma_start(out=dst, in_=ckvo[:]).then_inc(s.sem("sk"), 16)
                s.bump("sk", 16)
            for k in range(NT_OWN):
                g.wait_ge(s.sem("ve"), base["ve"] + k + 1)
                g.dma_start(out=s.lg_local[k * 128:(k + 1) * 128, :],
                            in_=s.log[:, k, :]).then_inc(s.sem("so"), 16)
                s.bump("so", 16)
            g.wait_ge(s.sem("so"), s.cnt["so"])
            if s.coll:
                g.collective_compute("AllGather", ALU.bypass,
                                     replica_groups=[list(range(8))],
                                     ins=[s.lg_local.opt()],
                                     outs=[s.lg_all.opt()]).then_inc(s.sem("cc"))
                s.bump("cc", 1)
            else:
                g.dma_start(out=s.lg_all[0:NTOK, :], in_=s.lg_local[:, :]
                            ).then_inc(s.sem("cc"), 16)
                s.bump("cc", 16)
            g.wait_ge(s.sem("ac_"), base["ac_"] + NT)
            g.wait_ge(s.sem("ta"), base["ta"] + NT)
            g.wait_ge(s.sem("tc"), base["tc"] + NT_OWN)
            g.wait_ge(s.sem("tb"), base["tb"] + 20)
            g.dma_start(out=self.wB_early,
                        in_=s.w_dqT.rearrange("(cc p) q -> p cc q", p=128)
                        ).then_inc(s.sem("sw"), 16)
            s.bump("sw", 16)
            g.wait_ge(s.sem("va"), base["va"] + NT)
            g.dma_start(out=s.xn_spill.rearrange("cc p t -> p cc t"),
                        in_=xnT_own[:]).then_inc(s.sem("so"), 16)
            s.bump("so", 16)
        block.gpsimd(f_gp)

        def f_vec(v):
            def V1(i):
                q4 = i % 4
                v.wait_ge(s.sem(SX[q4]), base[SX[q4]] + 16 * (i // 4 + 1))
                v.reduce_sum(s.sm(0, 16 * q4), xtile[:, q4], axis=AX.X)
                v.drain()
                v.tensor_scalar_mul(s.sm(0, 16 * q4), s.sm(0, 16 * q4), -1.0 / D)
                v.drain().then_inc(s.sem("vb"), 1)
                s.bump("vb")

            def V2(i):
                q4 = i % 4
                v.wait_ge(s.sem("aa"), base["aa"] + i + 1)
                v.tensor_scalar(s.sm(4, 16 * q4), s.sm(2, 16 * q4), 1.0 / D, EPS,
                                ALU.mult, ALU.add)
                v.drain().then_inc(s.sem("vc"), 1)
                s.bump("vc")

            def V3(i):
                q4 = i % 4
                v.wait_ge(s.sem("ab"), base["ab"] + i + 1)
                v.reciprocal(s.sm(8, 16 * q4), s.sm(6, 16 * q4))
                v.drain()
                v.tensor_tensor(s.sm(12, 16 * q4), s.sm(8, 16 * q4),
                                s.sm(8, 16 * q4), ALU.mult)
                v.drain()
                v.tensor_tensor(s.sm(12, 16 * q4), s.sm(12, 16 * q4),
                                s.sm(4, 16 * q4), ALU.mult)
                v.drain()
                v.tensor_scalar(s.sm(12, 16 * q4), s.sm(12, 16 * q4), -0.5, 1.5,
                                ALU.mult, ALU.add)
                v.drain()
                v.tensor_tensor(s.sm(8, 16 * q4), s.sm(8, 16 * q4),
                                s.sm(12, 16 * q4), ALU.mult)
                v.drain()
                v.tensor_tensor(s.sm(10, 16 * q4), s.sm(0, 16 * q4),
                                s.sm(8, 16 * q4), ALU.mult)
                v.drain().then_inc(s.sem("vd"), 1)
                s.bump("vd")

            def V4(i):
                own = i >= NT_SEQ
                k = i - NT_SEQ
                gi = i // 4
                v.wait_ge(s.sem("ta"), base["ta"] + i + 1)
                if not own:
                    if gi >= 1:
                        v.wait_ge(s.sem("tb"), base["tb"] + gi)
                    srcb = s.pp0.bitcast(BF16)
                    half = (i % 2) * 2048
                    v.tensor_copy(
                        xnTg[:, :, (i % 4) * 128:(i % 4) * 128 + 128],
                        srcb[:, half:half + 2048]
                            .rearrange("p (cc t) -> p cc t", t=128))
                    v.drain().then_inc(s.sem("va"), 1)
                    s.bump("va")
                else:
                    if k >= 2:
                        v.wait_ge(s.sem("tc"), base["tc"] + k - 1)
                    v.tensor_copy(xnTf[:, k % 2], s.pp0[:, :]
                                  .rearrange("p (cc t) -> p cc t", t=128))
                    v.drain()
                    v.tensor_copy(xnT_own[:, :, k * 128:(k + 1) * 128],
                                  xnTf[:, k % 2])
                    v.drain().then_inc(s.sem("va"), 1)
                    s.bump("va")
                    v.wait_ge(s.sem("tc"), base["tc"] + k + 1)
                    v.tensor_copy(s.log[:, k, :], s.pp1[:, 0:16])
                    v.drain().then_inc(s.sem("ve"), 1)
                    s.bump("ve")

            for i in range(NT + 3):
                if i < NT:
                    V1(i)
                if 1 <= i < NT + 1:
                    V2(i - 1)
                if 2 <= i < NT + 2:
                    V3(i - 2)
                if 3 <= i < NT + 3:
                    V4(i - 3)
            L = s.log[:]
            m1 = s.small[:, 16:32]
            m2 = s.small[:, 32:48]
            dd = s.small[:, 48:64]
            w1 = s.small[:, 64:80]
            w2 = s.small[:, 80:96]
            v.reduce_max(m1, L, axis=AX.X)
            v.drain()
            v.tensor_tensor(msk1[:], L, m1[:, :, None].broadcast_to((128, 16, 16)),
                            ALU.is_equal)
            v.drain()
            v.scalar_tensor_tensor(msk2[:], msk1[:], -1e38, L, ALU.mult, ALU.add)
            v.drain()
            v.reduce_max(m2, msk2[:], axis=AX.X)
            v.drain()
            v.tensor_tensor(msk2[:], msk2[:],
                            m2[:, :, None].broadcast_to((128, 16, 16)), ALU.is_equal)
            v.drain()
            v.tensor_tensor(dd, m1, m2, ALU.subtract)
            v.drain().then_inc(s.sem("vb"), 1)
            s.bump("vb")
            v.wait_ge(s.sem("ab"), base["ab"] + NT + 1)
            v.tensor_scalar(w2, w1, -1.0, 1.0, ALU.mult, ALU.add)
            v.drain()
            v.tensor_tensor(s.rw[:], msk1[:],
                            w1[:, :, None].broadcast_to((128, 16, 16)), ALU.mult)
            v.drain()
            v.tensor_tensor(msk1[:], msk2[:],
                            w2[:, :, None].broadcast_to((128, 16, 16)), ALU.mult)
            v.drain()
            v.tensor_tensor(s.rw[:], s.rw[:], msk1[:], ALU.add)
            v.drain().then_inc(s.sem("ve"), 1)
            s.bump("ve")
        block.vector(f_vec)

        def f_act(sc):
            def A1(i):
                q4 = i % 4
                sc.wait_ge(s.sem("vb"), base["vb"] + i + 1)
                sc.activation(out=sqs[:], in_=xtile[:, q4], func=AF.Square,
                              bias=s.sm(0, 16 * q4), scale=1.0,
                              accum_out=s.sm(2, 16 * q4))
                sc.drain().then_inc(s.sem("aa"), 1)
                s.bump("aa")

            def A2(i):
                q4 = i % 4
                sc.wait_ge(s.sem("vc"), base["vc"] + i + 1)
                sc.sqrt(s.sm(6, 16 * q4), s.sm(4, 16 * q4))
                sc.drain().then_inc(s.sem("ab"), 1)
                s.bump("ab")

            def A3(i):
                own = i >= NT_SEQ
                q4 = i % 4
                par = i % 2
                sc.wait_ge(s.sem("vd"), base["vd"] + i + 1)
                if not own and i >= 4:
                    sc.wait_ge(s.sem("ta"), base["ta"] + i - 3)
                if own and i >= 2:
                    sc.wait_ge(s.sem("ta"), base["ta"] + i - 1)
                dst = xnbf[:, q4] if not own else xnf32[:, par]
                sc.activation(out=dst, in_=xtile[:, q4], func=AF.Identity,
                              bias=s.sm(10, 16 * q4), scale=s.sm(8, 16 * q4))
                sc.drain().then_inc(s.sem("ac_"), 1)
                s.bump("ac_")

            def ACKV(i):
                gi = i // 4
                if gi >= 1:
                    sc.wait_ge(s.sem("sk"), base["sk"] + 16 * gi)
                for kc in range(4):
                    sc.wait_ge(s.sem("td"), base["td"] + 4 * gi + kc + 1)
                    sc.copy(ckvo[:, kc, :], s.pp1[:, kc * 512:(kc + 1) * 512])
                    sc.drain().then_inc(s.sem("ad"), 1)
                    s.bump("ad")

            for i in range(NT + 2):
                if i < NT:
                    A1(i)
                if 1 <= i < NT + 1:
                    A2(i - 1)
                if 2 <= i < NT + 2:
                    A3(i - 2)
                    if (i - 2) % 4 == 3:
                        ACKV(i - 2)
            sc.wait_ge(s.sem("vb"), base["vb"] + NT + 1)
            sc.activation(out=s.small[:, 64:80], in_=s.small[:, 48:64],
                          func=AF.Sigmoid)
            sc.drain().then_inc(s.sem("ab"), 1)
            s.bump("ab")
        block.scalar(f_act)

        def f_tens(t):
            t.wait_ge(s.sem("sw"), base["sw"] + 32)
            for i in range(NT):
                own = i >= NT_SEQ
                k = i - NT_SEQ
                par = i % 2
                gi = i // 4
                t.wait_ge(s.sem("ac_"), base["ac_"] + i + 1)
                if not own:
                    if i >= 2:
                        t.wait_ge(s.sem("va"), base["va"] + i - 1)
                    dstb = s.pp0.bitcast(BF16)
                    half = par * 2048
                    for cc in range(16):
                        ins = t.transpose(
                            dstb[:, half + cc * 128: half + (cc + 1) * 128],
                            xnbf[:, i % 4][:, cc * 128:(cc + 1) * 128], s.idb[:])
                    ins.then_inc(s.sem("ta"), 1)
                    s.bump("ta")
                else:
                    t.wait_ge(s.sem("va"), base["va"] + i)
                    for cc in range(16):
                        ins = t.transpose(s.pp0[:, cc * 128:(cc + 1) * 128],
                                          xnf32[:, par][:, cc * 128:(cc + 1) * 128],
                                          s.idf[:])
                    ins.then_inc(s.sem("ta"), 1)
                    s.bump("ta")
                    t.wait_ge(s.sem("va"), base["va"] + i + 1)
                    if k % 4 == 0:
                        t.wait_ge(s.sem("ad"), base["ad"] + 4 * gi)
                    for cc in range(16):
                        ins = t.matmul(s.pp1[:, 0:16], xnTf[:, k % 2][:, cc, :],
                                       gwt[:, cc, :], start=(cc == 0), stop=(cc == 15))
                    ins.then_inc(s.sem("tc"), 1)
                    s.bump("tc")
                if i % 4 == 3:
                    t.wait_ge(s.sem("va"), base["va"] + i + 1)
                    if own:
                        t.wait_ge(s.sem("ve"), base["ve"] + k + 1)
                    if gi >= 1:
                        t.wait_ge(s.sem("ad"), base["ad"] + 4 * gi)
                    for kc in range(4):
                        for cc in range(16):
                            if own:
                                k0 = k - 3
                                rhs = xnT_own[:, cc, k0 * 128: k0 * 128 + 512]
                            else:
                                rhs = xnTg[:, cc, :]
                            ins = t.matmul(s.pp1[:, kc * 512:(kc + 1) * 512],
                                           wdkv[:, cc, kc * 128:(kc + 1) * 128],
                                           rhs, start=(cc == 0), stop=(cc == 15))
                        ins.then_inc(s.sem("td"), 1)
                        s.bump("td")
                    t.drain().then_inc(s.sem("tb"), 1)
                    s.bump("tb")
        block.tensor(f_tens)

    def phase_B(self, block):
        s = self
        nc = s.nc
        sbat = lambda n, off, shape, dt: nc.alloc_sbuf_tensor_at(n, shape, dt, offset=off).ap()
        wB = self.wB_early
        cqT = sbat("b_cq", SLAB3, [128, 16, 2048], BF16)
        qev = sbat("b_qev", SLAB1, [128, 2, 512], BF16)
        xnT_own = self.A_bufs["xnT_own"]
        base = dict(s.cnt)
        PS = [s.pp0[:, 0:512], s.pp0[:, 512:1024],
              s.pp0[:, 1024:1536], s.pp0[:, 1536:2048]]

        def f_gp(g):
            g.wait_ge(s.sem("ta"), base["ta"] + 64)
            g.dma_start(out=wB[:],
                        in_=s.w_uqT.rearrange("(cc p) q -> p cc q", p=128)
                        ).then_inc(s.sem("sw"), 16)
            s.bump("sw", 16)
        block.gpsimd(f_gp)

        def f_tens(t):
            t.wait_ge(s.sem("sw"), base["sw"])
            t.wait_ge(s.sem("va"), base["va"])
            for n in range(64):
                g_, qc = divmod(n, 16)
                if n >= 4:
                    t.wait_ge(s.sem("va"), base["va"] + n - 3)
                for cc in range(16):
                    ins = t.matmul(PS[n % 4], wB[:, cc, qc * 128:(qc + 1) * 128],
                                   xnT_own[:, cc, g_ * 512:(g_ + 1) * 512],
                                   start=(cc == 0), stop=(cc == 15))
                ins.then_inc(s.sem("ta"), 1)
                s.bump("ta")
            t.wait_ge(s.sem("sw"), base["sw"] + 16)
            for n in range(64):
                g_, qc = divmod(n, 16)
                if n < 4:
                    t.wait_ge(s.sem("va"), base["va"] + 64)
                else:
                    t.wait_ge(s.sem("vb"), base["vb"] + n - 3)
                for cc in range(16):
                    ins = t.matmul(PS[n % 4], wB[:, cc, qc * 128:(qc + 1) * 128],
                                   cqT[:, cc, g_ * 512:(g_ + 1) * 512],
                                   start=(cc == 0), stop=(cc == 15))
                ins.then_inc(s.sem("tc"), 1)
                s.bump("tc")
        block.tensor(f_tens)

        def f_vec(v):
            v.wait_ge(s.sem("sk"), base["sk"])
            v.wait_ge(s.sem("so"), base["so"])
            for n in range(64):
                g_, qc = divmod(n, 16)
                v.wait_ge(s.sem("ta"), base["ta"] + n + 1)
                v.tensor_copy(cqT[:, qc, g_ * 512:(g_ + 1) * 512], PS[n % 4])
                v.drain().then_inc(s.sem("va"), 1)
                s.bump("va")
            for n in range(64):
                par = n % 2
                v.wait_ge(s.sem("tc"), base["tc"] + n + 1)
                if n >= 2:
                    v.wait_ge(s.sem("sq" if par == 0 else "sq1"),
                              base["sq" if par == 0 else "sq1"] + 16 * (n // 2))
                v.tensor_copy(qev[:, par], PS[n % 4])
                v.drain().then_inc(s.sem("vb"), 1)
                s.bump("vb")
        block.vector(f_vec)

        def f_sync(sy):
            for n in range(64):
                g_, qc = divmod(n, 16)
                par = n % 2
                sy.wait_ge(s.sem("vb"), base["vb"] + n + 1)
                sy.dma_start(out=s.qT_spill[qc, :, g_ * 512:(g_ + 1) * 512],
                             in_=qev[:, par]
                             ).then_inc(s.sem("sq" if par == 0 else "sq1"), 16)
                s.bump("sq" if par == 0 else "sq1", 16)
        block.sync(f_sync)

    def phase_thresh(self, block):
        s = self
        nc = s.nc
        sbat = lambda n, off, shape, dt: nc.alloc_sbuf_tensor_at(n, shape, dt, offset=off).ap()
        lg = sbat("t_lg", SLAB2, [128, 128, 16], F32)
        mske = sbat("t_mske", SLAB2 + 8 * KB, [128, 128, 16], F32)
        mskd = sbat("t_mskd", SLAB2 + 16 * KB, [128, 128, 16], F32)
        msk2 = sbat("t_msk2", SLAB2 + 24 * KB, [128, 128, 16], F32)
        rw8 = sbat("t_rw8", SLAB2 + 32 * KB, [128, 128, 8], F32)
        cmp = sbat("t_cmp", SLAB2 + 36 * KB, [128, 128, 8], F32)
        cntR = sbat("t_cntR", SLAB2 + 40 * KB, [128, 128, 8], F32)
        cnt2 = sbat("t_cnt2", SLAB2 + 44 * KB, [128, 8], F32)
        mm1 = sbat("t_m1", SLAB2 + 45 * KB, [128, 128], F32)
        mm2 = sbat("t_m2", SLAB2 + 46 * KB, [128, 128], F32)
        ww1 = sbat("t_w1", SLAB2 + 47 * KB, [128, 128], F32)
        ww2 = sbat("t_w2", SLAB2 + 48 * KB, [128, 128], F32)
        st8 = sbat("t_st8", SLAB2 + 49 * KB, [128, 8, 8], F32)
        ones128 = sbat("t_ones", SLAB2 + 50 * KB, [128, 128], F32)
        snap = sbat("t_snap", SLAB2 + 51 * KB, [128, 96], F32)
        base = dict(s.cnt)

        def f_gp(g):
            g.wait_ge(s.sem("cc"), base["cc"])
            g.wait_ge(s.sem("tc"), base["tc"])
            g.dma_start(out=lg[:],
                        in_=s.lg_all.rearrange("(p t) e -> p t e", p=128)
                        ).then_inc(s.sem("so"), 16)
            s.bump("so", 16)
        block.gpsimd(f_gp)


        def f_act(sc):
            sc.wait_ge(s.sem("vb"), base["vb"] + 1)
            sc.activation(out=ww1[:], in_=mm1[:], func=AF.Sigmoid)
            sc.drain().then_inc(s.sem("ab"), 1)
            s.bump("ab")
        block.scalar(f_act)

        def f_tens(t):
            for it in range(BIS_IT):
                t.wait_ge(s.sem("vf"), base["vf"] + it + 1)
                ins = t.matmul(s.pp0[:, 0:8], ones128[:], cnt2[:],
                               start=True, stop=True)
                ins.then_inc(s.sem("tb"), 1)
                s.bump("tb")
        block.tensor(f_tens)

        def f_vec(v):
            v.wait_ge(s.sem("so"), base["so"] + 16)
            v.reduce_max(mm1[:], lg[:], axis=AX.X)
            v.drain()
            v.tensor_tensor(mske[:], lg[:],
                            mm1[:, :, None].broadcast_to((128, 128, 16)), ALU.is_equal)
            v.drain()
            v.scalar_tensor_tensor(mskd[:], mske[:], -1e38, lg[:], ALU.mult, ALU.add)
            v.drain()
            v.reduce_max(mm2[:], mskd[:], axis=AX.X)
            v.drain()
            v.tensor_tensor(msk2[:], mskd[:],
                            mm2[:, :, None].broadcast_to((128, 128, 16)), ALU.is_equal)
            v.drain()
            v.tensor_tensor(mm1[:], mm1[:], mm2[:], ALU.subtract)
            v.drain().then_inc(s.sem("vb"), 1)
            s.bump("vb")
            v.wait_ge(s.sem("ab"), base["ab"] + 1)
            v.tensor_scalar(ww2[:], ww1[:], -1.0, 1.0, ALU.mult, ALU.add)
            v.drain()
            v.tensor_tensor(rw8[:], mske[:, :, 8:16],
                            ww1[:, :, None].broadcast_to((128, 128, 8)), ALU.mult)
            v.drain()
            v.tensor_tensor(cmp[:], msk2[:, :, 8:16],
                            ww2[:, :, None].broadcast_to((128, 128, 8)), ALU.mult)
            v.drain()
            v.tensor_tensor(rw8[:], rw8[:], cmp[:], ALU.add)
            v.drain()
            v.memset(st8[:, 0, :], 0.0)
            v.memset(st8[:, 2, :], 1.0)
            v.memset(ones128[:], 1.0)
            v.drain()
            for it in range(BIS_IT):
                par = it % 2
                lo, hi = st8[:, 0 + par, :], st8[:, 2 + par, :]
                lo_n, hi_n = st8[:, 1 - par, :], st8[:, 3 - par, :]
                mid, ge, c8 = st8[:, 4, :], st8[:, 5, :], st8[:, 6, :]
                v.tensor_tensor(mid, lo, hi, ALU.add)
                v.drain()
                v.tensor_scalar_mul(mid, mid, 0.5)
                v.drain()
                v.tensor_tensor(cmp[:], rw8[:],
                                mid[:, None, :].broadcast_to((128, 128, 8)), ALU.is_gt)
                v.drain()
                v.reduce_sum(cnt2[:], cmp[:].rearrange("p t e -> p e t"), axis=AX.X)
                v.drain().then_inc(s.sem("vf"), 1)
                s.bump("vf")
                v.wait_ge(s.sem("tb"), base["tb"] + it + 1)
                v.tensor_single_scalar(ge, s.pp0[:, 0:8], float(CAP), ALU.is_ge)
                v.drain()
                if s.debug and it in (0, 1, 2):
                    v.tensor_copy(snap[:, it * 24: it * 24 + 8], s.pp0[:, 0:8])
                    v.drain()
                    v.tensor_copy(snap[:, it * 24 + 8: it * 24 + 16], mid)
                    v.drain()
                    v.tensor_copy(snap[:, it * 24 + 16: it * 24 + 24], ge)
                    v.drain()
                tmp = st8[:, 7, :]
                v.tensor_tensor(tmp, mid, lo, ALU.subtract)
                v.drain()
                v.tensor_tensor(tmp, tmp, ge, ALU.mult)
                v.drain()
                v.tensor_tensor(lo_n, lo, tmp, ALU.add)
                v.drain()
                v.tensor_tensor(tmp, hi, mid, ALU.subtract)
                v.drain()
                v.tensor_tensor(tmp, tmp, ge, ALU.mult)
                v.drain()
                v.tensor_tensor(hi_n, mid, tmp, ALU.add)
                v.drain()
            lo_fin = st8[:, 0 + BIS_IT % 2, :]
            v.tensor_copy(s.wgt[:, :, 0:8], s.rw[:, :, 0:8])
            v.drain()
            v.tensor_tensor(cmp[:, 0:16, :], s.rw[:, :, 8:16],
                            lo_fin[:, None, :].broadcast_to((128, 16, 8)), ALU.is_gt)
            v.drain()
            v.tensor_tensor(s.wgt[:, :, 8:16], s.rw[:, :, 8:16], cmp[:, 0:16, :],
                            ALU.mult)
            v.drain().then_inc(s.sem("ve"), 1)
            s.bump("ve")
        block.vector(f_vec)

        if s.debug:
            def f_gp2(g):
                g.wait_ge(s.sem("ve"), s.cnt["ve"])
                g.dma_start(out=s.bis_dbg[:, :], in_=snap[:]
                            ).then_inc(s.sem("so"), 16)
                s.bump("so", 16)
                g.dma_start(out=s.thr_dbg[:, :], in_=st8[:, 0 + BIS_IT % 2, :]
                            ).then_inc(s.sem("so"), 16)
                g.dma_start(out=s.wgt_dbg[:, :],
                            in_=s.wgt[:].rearrange("p a b -> p (a b)")
                            ).then_inc(s.sem("so"), 16)
                s.bump("so", 32)
            block.gpsimd(f_gp2)

    def _attn_items(self):
        items = []
        for i in range(6):
            items.append((0, "full", i, None))
        items.append((0, "diag", (0, 0), 0))
        items.append((0, "diag", (0, 1), 512))
        for i in range(14):
            items.append((1, "full", i, None))
        items.append((1, "diag", (1, 0), 0))
        items.append((1, "diag", (1, 1), 512))
        return items

    def phase_attn(self, block):
        s = self
        nc = s.nc
        sbat = lambda n, off, shape, dt: nc.alloc_sbuf_tensor_at(n, shape, dt, offset=off).ap()
        qh = sbat("at_qh", SLAB1, [128, 16, 1024], BF16)
        ckvi = sbat("at_ckv", SLAB1 + 32 * KB, [128, 2, 4, 512], BF16)
        pT = sbat("at_pt", SLAB1 + 40 * KB, [128, 2, 1024], BF16)
        sums = sbat("at_sums", SLAB1 + 44 * KB, [128, 1024], F32)
        rsr = sbat("at_rsr", SLAB1 + 48 * KB, [128, 2, 1024], F32)
        ctxe = sbat("at_ctxe", SLAB1 + 56 * KB, [128, 2, 1024], BF16)
        wuk = sbat("at_wuk", SLAB2, [128, 4, 2048], BF16)
        wuv = sbat("at_wuv", SLAB2 + 16 * KB, [128, 4, 2048], BF16)
        kT = sbat("at_kt", SLAB2 + 32 * KB, [128, 16, 512], BF16)
        vblk = sbat("at_v", SLAB2 + 48 * KB, [128, 4, 2048], BF16)
        ctx = sbat("at_ctx", SLAB3, [128, 16, 1024], F32)
        base = dict(s.cnt)
        items = self._attn_items()
        UPQ = [s.pp0[:, 0:512], s.pp0[:, 512:1024],
               s.pp0[:, 1024:1536], s.pp0[:, 1536:2048]]
        SS = [s.pp0[:, 0:1024], s.pp0[:, 1024:2048]]
        CTXPS = s.pp1[:, 0:1024]
        XPS = s.pp1[:, 1024:2048]

        NIT = len(items)
        dig = [it for it, (p, knd, _, _) in enumerate(items) if knd == "diag"]

        def f_gp(g):
            for p in range(2):
                if p == 0:
                    g.wait_ge(s.sem("sq"), base["sq"])
                    g.wait_ge(s.sem("sq1"), base["sq1"])
                    g.wait_ge(s.sem("ve"), base["ve"])
                    g.wait_ge(s.sem("so"), base["so"])
                    g.dma_start(out=wuk[:],
                                in_=s.w_ukT.rearrange("(kc p) d -> p kc d", p=128)
                                ).then_inc(s.sem("sw"), 16)
                    s.bump("sw", 16)
                    g.dma_start(out=wuv[:],
                                in_=s.w_uvT.rearrange("(kc p) d -> p kc d", p=128)
                                ).then_inc(s.sem("sw"), 16)
                    s.bump("sw", 16)
                else:
                    g.wait_ge(s.sem("tc"), base["tc"] + 64 * 8)
                g.dma_start(out=qh[:],
                            in_=s.qT_spill[:, :, p * 1024:(p + 1) * 1024]
                            .rearrange("dc p t -> p dc t")
                            ).then_inc(s.sem("sw1"), 16)
                s.bump("sw1", 16)
        block.gpsimd(f_gp)

        def f_sync(sy):
            for it, (p, knd, src, rel) in enumerate(items):
                par = it % 2
                if it < 2:
                    sy.wait_ge(s.sem("ve"), base["ve"])
                if it >= 2:
                    sy.wait_ge(s.sem("td"), base["td"] + 32 * (it - 1))
                if knd == "full":
                    srcap = s.ckv_spill[src]
                else:
                    srcap = s.ckv_own[src[0], src[1]]
                sy.dma_start(out=ckvi[:, par],
                             in_=srcap.rearrange("kc p t -> p kc t")
                             ).then_inc(s.sem("sy" if par == 0 else "sy1"), 16)
                s.bump("sy" if par == 0 else "sy1", 16)
        block.sync(f_sync)

        def f_tens(t):
            t.wait_ge(s.sem("sw"), base["sw"] + 32)
            for it, (p, knd, src, rel) in enumerate(items):
                par = it % 2
                first_of_pass = (it == 0 or it == 8)
                t.wait_ge(s.sem("sy" if par == 0 else "sy1"),
                          base["sy" if par == 0 else "sy1"] + 16 * (it // 2 + 1))
                t.wait_ge(s.sem("sw1"), base["sw1"] + 16 * (p + 1))
                for dc in range(16):
                    q = dc % 4
                    if it > 0 or dc >= 4:
                        pass
                    if it == 0 and dc == 0:
                        t.wait_ge(s.sem("vb"), base["vb"])
                    if dc >= 4:
                        t.wait_ge(s.sem("ad"), base["ad"] + 16 * it + dc - 3)
                    elif it > 0:
                        t.wait_ge(s.sem("ab"), base["ab"] + 64 * it)
                    for kc in range(4):
                        ins = t.matmul(UPQ[q][:, 0:512],
                                       wuk[:, kc, dc * 128:(dc + 1) * 128],
                                       ckvi[:, par, kc, :],
                                       start=(kc == 0), stop=(kc == 3))
                    ins.then_inc(s.sem("td"), 1)
                    s.bump("td")
                for n in range(16):
                    sub, dg = divmod(n, 4)
                    q = n % 4
                    if n >= 4:
                        t.wait_ge(s.sem("va"), base["va"] + 16 * it + n - 3)
                    else:
                        t.wait_ge(s.sem("ad"), base["ad"] + 16 * it + 13 + n)
                    for kc in range(4):
                        ins = t.matmul(UPQ[q][:, 0:512],
                                       ckvi[:, par, kc, sub * 128:(sub + 1) * 128],
                                       wuv[:, kc, dg * 512:(dg + 1) * 512],
                                       start=(kc == 0), stop=(kc == 3))
                    ins.then_inc(s.sem("td"), 1)
                    s.bump("td")
                for n_ in range(65):
                    if n_ < 64:
                        h, kt = divmod(n_, 4)
                        sl = n_ % 2
                        if n_ == 0:
                            t.wait_ge(s.sem("ad"), base["ad"] + 16 * (it + 1))
                            t.wait_ge(s.sem("va"), base["va"] + 16 * (it + 1))
                        prev_user = 64 * it + n_ - 2
                        if prev_user >= 0:
                            t.wait_ge(s.sem("ab"), base["ab"] + prev_user + 1)
                        for half in range(2):
                            ins = t.matmul(SS[sl][:, half * 512:(half + 1) * 512],
                                           kT[:, h, kt * 128:(kt + 1) * 128],
                                           qh[:, h, half * 512:(half + 1) * 512],
                                           start=True, stop=True)
                        ins.then_inc(s.sem("tc"), 1)
                        s.bump("tc")
                    if n_ >= 1:
                        m_ = n_ - 1
                        h, kt = divmod(m_, 4)
                        nn_ = 64 * it + m_
                        if knd == "diag":
                            t.wait_ge(s.sem("vd"), base["vd"] + dig.index(it) * 64
                                      + m_ + 1)
                        else:
                            t.wait_ge(s.sem("ab"), base["ab"] + nn_ + 1)
                        if kt == 0:
                            hh = 16 * it + h
                            if hh > 0:
                                t.wait_ge(s.sem("vc"), base["vc"] + hh)
                        for half in range(2):
                            ins = t.matmul(CTXPS[:, half * 512:(half + 1) * 512],
                                           vblk[:, kt, h * 128:(h + 1) * 128],
                                           pT[:, nn_ % 2, half * 512:(half + 1) * 512],
                                           start=(kt == 0), stop=(kt == 3))
                        if m_ == 0 and it > 0:
                            t.wait_ge(s.sem("ve"), base["ve"] + it)
                        for half in range(2):
                            ins = t.matmul(XPS[:, half * 512:(half + 1) * 512],
                                           s.eh[:, h, :],
                                           pT[:, nn_ % 2, half * 512:(half + 1) * 512],
                                           start=(m_ == 0),
                                           stop=(m_ == 63))
                        ins.then_inc(s.sem("te"), 1)
                        s.bump("te")
        block.tensor(f_tens)

        def f_act(sc):
            for it, (p, knd, src, rel) in enumerate(items):
                for dc in range(16):
                    q = dc % 4
                    sc.wait_ge(s.sem("td"), base["td"] + 32 * it + dc + 1)
                    sc.copy(kT[:, dc, :], UPQ[q][:, 0:512])
                    sc.drain().then_inc(s.sem("ad"), 1)
                    s.bump("ad")
                for h in range(NH):
                    for kt in range(4):
                        nn_ = 64 * it + h * 4 + kt
                        sl = (h * 4 + kt) % 2
                        sc.wait_ge(s.sem("tc"), base["tc"] + nn_ + 1)
                        if nn_ >= 2:
                            sc.wait_ge(s.sem("te"), base["te"] + nn_ - 1)
                        sc.activation(out=pT[:, nn_ % 2, :], in_=SS[sl][:, :],
                                      func=AF.Exp, scale=SCALE,
                                      bias=s.biass[:, it:it + 1])
                        sc.drain().then_inc(s.sem("ab"), 1)
                        s.bump("ab")
        block.scalar(f_act)

        def f_vec(v):
            for it, (p, knd, src, rel) in enumerate(items):
                par = it % 2
                first_of_pass = (it == 0 or it == 8)
                for n in range(16):
                    sub, dg = divmod(n, 4)
                    q = n % 4
                    v.wait_ge(s.sem("td"), base["td"] + 32 * it + 16 + n + 1)
                    v.tensor_copy(vblk[:, sub, dg * 512:(dg + 1) * 512],
                                  UPQ[q][:, 0:512])
                    v.drain().then_inc(s.sem("va"), 1)
                    s.bump("va")
                for h in range(NH):
                    if knd == "diag":
                        for kt in range(4):
                            nn_ = 64 * it + h * 4 + kt
                            inv = rel + kt * 128
                            v.wait_ge(s.sem("ab"), base["ab"] + nn_ + 1)
                            if inv > 0:
                                v.memset(pT[:, nn_ % 2, 0:inv], 0.0)
                                v.drain()
                            v.tensor_tensor(pT[:, nn_ % 2, inv:inv + 128],
                                            pT[:, nn_ % 2, inv:inv + 128],
                                            s.tril[:], ALU.mult)
                            v.drain().then_inc(s.sem("vd"), 1)
                            s.bump("vd")
                    v.wait_ge(s.sem("te"), base["te"] + 64 * it + 4 * h + 4)
                    dst = ctx[:, h, :]
                    if first_of_pass:
                        v.tensor_copy(dst, CTXPS[:, :])
                    else:
                        v.tensor_tensor(dst, dst, CTXPS[:, :], ALU.add)
                    v.drain().then_inc(s.sem("vc"), 1)
                    s.bump("vc")
                v.wait_ge(s.sem("te"), base["te"] + 64 * (it + 1))
                if first_of_pass:
                    v.tensor_copy(sums[:], XPS[:, :])
                else:
                    v.tensor_tensor(sums[:], sums[:], XPS[:, :], ALU.add)
                v.drain().then_inc(s.sem("ve"), 1)
                s.bump("ve")
                if it == 7 or it == 23:
                    v.reciprocal(sums[:], sums[:])
                    v.drain().then_inc(s.sem("vf"), 1)
                    s.bump("vf")
                    for h in range(NH):
                        v.wait_ge(s.sem("so"),
                                  base["so"] + 16 * (33 * p + 2 * h + 2))
                        v.tensor_tensor(ctxe[:, h % 2, :], ctx[:, h, :],
                                        rsr[:, h % 2, :], ALU.mult)
                        v.drain().then_inc(s.sem("vb"), 1)
                        s.bump("vb")
        block.vector(f_vec)

        def f_gp2(g):
            for p in range(2):
                g.wait_ge(s.sem("vf"), base["vf"] + p + 1)
                g.dma_start(out=s.rs_dram[p], in_=sums[0:16, :]
                            ).then_inc(s.sem("so"), 16)
                s.bump("so", 16)
                for h in range(NH):
                    g.wait_ge(s.sem("so"), s.cnt["so"])
                    if h >= 2:
                        g.wait_ge(s.sem("vb"), base["vb"] + 16 * p + h - 1)
                    g.dma_start(out=rsr[:, h % 2, :],
                                in_=s.rs_dram[p, h, :].partition_broadcast(128)
                                ).then_inc(s.sem("so"), 16)
                    s.bump("so", 16)
                    g.wait_ge(s.sem("vb"), base["vb"] + 16 * p + h + 1)
                    g.dma_start(out=s.ctx_spill[p, h],
                                in_=ctxe[:, h % 2, :]).then_inc(s.sem("so"), 16)
                    s.bump("so", 16)
        block.gpsimd(f_gp2)

    def phase_ffn1(self, block):
        s = self
        nc = s.nc
        sbat = lambda n, off, shape, dt: nc.alloc_sbuf_tensor_at(n, shape, dt, offset=off).ap()
        xnT = sbat("f1_xnT", SLAB1, [128, 16, 2048], BF16)
        ug = sbat("f1_ug", SLAB2, [128, 2, 2, 16, 512], BF16)
        ire = sbat("f1_ir", SLAB3, [128, 2, 4, 2048], BF16)
        sil = sbat("f1_sil", SLAB3 + 32 * KB, [128, 2, 512], F32)
        base = dict(s.cnt)

        def f_gp(g):
            g.wait_ge(s.sem("so"), base["so"])
            g.wait_ge(s.sem("tc"), base["tc"])
            g.wait_ge(s.sem("te"), base["te"])
            g.wait_ge(s.sem("vb"), base["vb"])
            g.dma_start(out=xnT[:],
                        in_=s.xn_spill.rearrange("cc p t -> p cc t")
                        ).then_inc(s.sem("so"), 16)
            s.bump("so", 16)
            for e in range(NE + 1):
                if e < NE:
                    par = e % 2
                    if e >= 2:
                        g.wait_ge(s.sem("tc"), base["tc"] + 32 * (e - 1))
                    g.dma_start(out=ug[:, par, 0],
                                in_=s.uT[:, e * EW:(e + 1) * EW]
                                .rearrange("(cc p) w -> p cc w", p=128)
                                ).then_inc(s.sem("sw" if par == 0 else "sw1"), 16)
                    g.dma_start(out=ug[:, par, 1],
                                in_=s.gT[:, e * EW:(e + 1) * EW]
                                .rearrange("(cc p) w -> p cc w", p=128)
                                ).then_inc(s.sem("sw" if par == 0 else "sw1"), 16)
                    s.bump("sw" if par == 0 else "sw1", 32)
                if e >= 1:
                    ep = e - 1
                    parp = ep % 2
                    g.wait_ge(s.sem("va"), base["va"] + 16 * (ep + 1))
                    g.dma_start(out=s.ir_spill[ep].rearrange("wc p t -> p wc t"),
                                in_=ire[:, parp]
                                ).then_inc(s.sem("sk" if parp == 0 else "sk1"), 16)
                    s.bump("sk" if parp == 0 else "sk1", 16)
        block.gpsimd(f_gp)

        def f_tens(t):
            t.wait_ge(s.sem("so"), base["so"] + 16)
            for e in range(NE):
                par = e % 2
                t.wait_ge(s.sem("sw" if par == 0 else "sw1"),
                          base["sw" if par == 0 else "sw1"] + 32 * (e // 2 + 1))
                for n in range(16):
                    tg, wc = divmod(n, 4)
                    nn_ = 32 * e + 2 * n
                    slg = s.pp0[:, (n % 2) * 1024:(n % 2) * 1024 + 512]
                    slu = s.pp1[:, (n % 2) * 1024:(n % 2) * 1024 + 512]
                    if nn_ >= 4:
                        t.wait_ge(s.sem("ab"), base["ab"] + 16 * e + n - 1)
                    for cc in range(16):
                        ins = t.matmul(slg, ug[:, par, 1, cc, wc * 128:(wc + 1) * 128],
                                       xnT[:, cc, tg * 512:(tg + 1) * 512],
                                       start=(cc == 0), stop=(cc == 15))
                    ins.then_inc(s.sem("tc"), 1)
                    s.bump("tc")
                    if nn_ >= 4:
                        t.wait_ge(s.sem("va"), base["va"] + 16 * e + n - 1)
                    for cc in range(16):
                        ins = t.matmul(slu, ug[:, par, 0, cc, wc * 128:(wc + 1) * 128],
                                       xnT[:, cc, tg * 512:(tg + 1) * 512],
                                       start=(cc == 0), stop=(cc == 15))
                    ins.then_inc(s.sem("tc"), 1)
                    s.bump("tc")
        block.tensor(f_tens)

        def f_act(sc):
            for e in range(NE):
                for n in range(16):
                    slg = s.pp0[:, (n % 2) * 1024:(n % 2) * 1024 + 512]
                    sc.wait_ge(s.sem("tc"), base["tc"] + 32 * e + 2 * n + 1)
                    if 16 * e + n >= 2:
                        sc.wait_ge(s.sem("va"), base["va"] + 16 * e + n - 1)
                    sc.activation(out=sil[:, n % 2, :], in_=slg, func=AF.Silu)
                    sc.drain().then_inc(s.sem("ab"), 1)
                    s.bump("ab")
        block.scalar(f_act)

        def f_vec(v):
            for e in range(NE):
                par = e % 2
                for n in range(16):
                    tg, wc = divmod(n, 4)
                    slu = s.pp1[:, (n % 2) * 1024:(n % 2) * 1024 + 512]
                    v.wait_ge(s.sem("tc"), base["tc"] + 32 * e + 2 * n + 2)
                    v.wait_ge(s.sem("ab"), base["ab"] + 16 * e + n + 1)
                    if n == 0 and e >= 2:
                        ps = "sk" if par == 0 else "sk1"
                        v.wait_ge(s.sem(ps), base[ps] + 16 * (e // 2))
                    v.tensor_tensor(ire[:, par, wc, tg * 512:(tg + 1) * 512],
                                    slu, sil[:, n % 2, :], ALU.mult)
                    v.drain().then_inc(s.sem("va"), 1)
                    s.bump("va")
        block.vector(f_vec)

    def phase_ffn2(self, block):
        s = self
        nc = s.nc
        sbat = lambda n, off, shape, dt: nc.alloc_sbuf_tensor_at(n, shape, dt, offset=off).ap()
        acc0 = sbat("f2_acc0", SLAB1, [128, 8, 2048], F32)
        acc1 = sbat("f2_acc1", SLAB3, [128, 8, 2048], F32)
        dte = sbat("f2_dt", SLAB2, [128, 2, 4, 2048], BF16)
        iri = sbat("f2_ir", SLAB2 + 32 * KB, [128, 2, 4, 2048], BF16)
        base = dict(s.cnt)
        self.F2_bufs = dict(acc0=acc0, acc1=acc1)

        def f_gp(g):
            g.wait_ge(s.sem("tc"), base["tc"])
            for e in range(NE):
                par = e % 2
                if e >= 2:
                    g.wait_ge(s.sem("tc"), base["tc"] + 16 * (e - 1))
                g.dma_start(out=dte[:, par],
                            in_=s.dT[e * EW:(e + 1) * EW, :]
                            .rearrange("(wc p) h -> p wc h", p=128)
                            ).then_inc(s.sem("sw" if par == 0 else "sw1"), 16)
                s.bump("sw" if par == 0 else "sw1", 16)
        block.gpsimd(f_gp)

        def f_sync(sy):
            sy.wait_ge(s.sem("tc"), base["tc"])
            sy.wait_ge(s.sem("sk"), base["sk"])
            sy.wait_ge(s.sem("sk1"), base["sk1"])
            for e in range(NE):
                par = e % 2
                if e >= 2:
                    sy.wait_ge(s.sem("tc"), base["tc"] + 16 * (e - 1))
                sy.dma_start(out=iri[:, par],
                             in_=s.ir_spill[e].rearrange("wc p t -> p wc t")
                             ).then_inc(s.sem("sy" if par == 0 else "sy1"), 16)
                s.bump("sy" if par == 0 else "sy1", 16)
        block.sync(f_sync)

        def f_tens(t):
            for e in range(NE):
                par = e % 2
                wsem = "sw" if par == 0 else "sw1"
                ksem = "sy" if par == 0 else "sy1"
                t.wait_ge(s.sem(wsem), base[wsem] + 16 * (e // 2 + 1))
                t.wait_ge(s.sem(ksem), base[ksem] + 16 * (e // 2 + 1))
                for ti in range(16):
                    n = 16 * e + ti
                    ps = s.pp0 if n % 2 == 0 else s.pp1
                    if n >= 2:
                        t.wait_ge(s.sem("va"), base["va"] + n - 1)
                    for hg in range(4):
                        for wc in range(4):
                            ins = t.matmul(ps[:, hg * 512:(hg + 1) * 512],
                                           iri[:, par, wc, ti * 128:(ti + 1) * 128],
                                           dte[:, par, wc, hg * 512:(hg + 1) * 512],
                                           start=(wc == 0), stop=(wc == 3))
                    ins.then_inc(s.sem("tc"), 1)
                    s.bump("tc")
        block.tensor(f_tens)

        def f_vec(v):
            for e in range(NE):
                for ti in range(16):
                    n = 16 * e + ti
                    ps = s.pp0 if n % 2 == 0 else s.pp1
                    acc = acc0 if ti < 8 else acc1
                    sl = acc[:, ti % 8, :]
                    wcol = s.wgt[:, ti, e:e + 1]
                    v.wait_ge(s.sem("tc"), base["tc"] + n + 1)
                    if e == 0:
                        v.tensor_scalar(sl, ps[:, :], wcol, None, ALU.mult)
                    else:
                        v.scalar_tensor_tensor(sl, ps[:, :], wcol, sl,
                                               ALU.mult, ALU.add)
                    v.drain().then_inc(s.sem("va"), 1)
                    s.bump("va")
        block.vector(f_vec)

    def phase_final(self, block):
        s = self
        nc = s.nc
        sbat = lambda n, off, shape, dt: nc.alloc_sbuf_tensor_at(n, shape, dt, offset=off).ap()
        wo = sbat("fn_wo", SLAB2, [128, 16, 1024], BF16)
        ctxt = sbat("fn_ctx", SLAB2 + 32 * KB, [128, 2, 16, 128], BF16)
        xfin = sbat("fn_x", SLAB2 + 40 * KB, [128, 2, 1024], F32)
        outt = sbat("fn_out", SLAB2 + 48 * KB, [128, 2, 1024], F32)
        outt2 = sbat("fn_out2", SLAB2 + 56 * KB, [128, 2, 1024], F32)
        bo = sbat("fn_bo", OFF_EH, [1, 2048], BF16)
        acc0 = self.F2_bufs["acc0"]
        acc1 = self.F2_bufs["acc1"]
        base = dict(s.cnt)

        def f_gp(g):
            g.wait_ge(s.sem("te"), base["te"])
            g.wait_ge(s.sem("tc"), base["tc"])
            g.dma_start(out=bo[:], in_=s.b_o[:, :]).then_inc(s.sem("sw"), 16)
            s.bump("sw", 16)
            for ch in range(2):
                if ch == 1:
                    g.wait_ge(s.sem("tc"), base["tc"] + 16)
                g.dma_start(out=wo[:],
                            in_=s.w_oT.rearrange("(dc p) c -> p dc c", p=128)
                            [:, :, ch * 1024:(ch + 1) * 1024]
                            ).then_inc(s.sem("sw1"), 16)
                s.bump("sw1", 16)
        block.gpsimd(f_gp)

        def f_act(sc):
            for ti in range(16):
                sc.wait_ge(s.sem("va"), base["va"] - 16 + ti + 1)
                acc = acc0 if ti < 8 else acc1
                sc.activation(out=acc[:, ti % 8, :], in_=acc[:, ti % 8, :],
                              func=AF.Silu)
                sc.drain().then_inc(s.sem("ab"), 1)
                s.bump("ab")
        block.scalar(f_act)

        def emit_out_dma(sy, m):
            mpar = m % 2
            ch_, ti_ = divmod(m, 16)
            sy.wait_ge(s.sem("vb"), base["vb"] + m + 1)
            sy.dma_start(out=s.out[ti_ * 128:(ti_ + 1) * 128,
                                   ch_ * 1024:(ch_ + 1) * 1024],
                         in_=outt2[:, mpar, :]
                         ).then_inc(s.sem("sy" if mpar == 0 else "sy1"), 16)
            s.bump("sy" if mpar == 0 else "sy1", 16)

        def f_sync(sy):
            sy.wait_ge(s.sem("tc"), base["tc"])
            for ch in range(2):
                for ti in range(16):
                    n = ch * 16 + ti
                    par = n % 2
                    if n >= 2:
                        emit_out_dma(sy, n - 2)
                        sy.wait_ge(s.sem("tc"), base["tc"] + n - 1)
                    p = ti // 8
                    sy.dma_start(out=ctxt[:, par],
                                 in_=s.ctx_spill[p, :, :, (ti % 8) * 128:
                                                 (ti % 8) * 128 + 128]
                                 .rearrange("dc p t -> p dc t")
                                 ).then_inc(s.sem("sq" if par == 0 else "sq1"), 16)
                    s.bump("sq" if par == 0 else "sq1", 16)
                    if n >= 2:
                        sy.wait_ge(s.sem("vb"), base["vb"] + n - 1)
                    sy.dma_start(out=xfin[:, par, :],
                                 in_=s.x_own[ti * 128:(ti + 1) * 128,
                                             ch * 1024:(ch + 1) * 1024]
                                 ).then_inc(s.sem("sx" if par == 0 else "sx1"), 16)
                    s.bump("sx" if par == 0 else "sx1", 16)
            emit_out_dma(sy, 30)
            emit_out_dma(sy, 31)
        block.sync(f_sync)

        def f_tens(t):
            t.wait_ge(s.sem("sw"), base["sw"] + 16)
            t.wait_ge(s.sem("va"), base["va"])
            for ch in range(2):
                t.wait_ge(s.sem("sw1"), base["sw1"] + 16 * (ch + 1))
                for ti in range(16):
                    n = ch * 16 + ti
                    par = n % 2
                    ps = (s.pp0 if par == 0 else s.pp1)[:, 0:1024]
                    t.wait_ge(s.sem("sq" if par == 0 else "sq1"),
                              base["sq" if par == 0 else "sq1"] + 16 * (n // 2 + 1))
                    if n >= 2:
                        t.wait_ge(s.sem("vb"), base["vb"] + n - 1)
                    for dc in range(16):
                        for half in range(2):
                            t.matmul(ps[:, half * 512:(half + 1) * 512],
                                     ctxt[:, par, dc, :],
                                     wo[:, dc, half * 512:(half + 1) * 512],
                                     start=(dc == 0), stop=False)
                    for half in range(2):
                        ins = t.matmul(ps[:, half * 512:(half + 1) * 512],
                                       s.ones_row[:],
                                       bo[0:1, ch * 1024 + half * 512:
                                          ch * 1024 + (half + 1) * 512],
                                       start=False, stop=True)
                    ins.then_inc(s.sem("tc"), 1)
                    s.bump("tc")
        block.tensor(f_tens)

        def f_vec(v):
            for ch in range(2):
                for ti in range(16):
                    n = ch * 16 + ti
                    par = n % 2
                    ps = (s.pp0 if par == 0 else s.pp1)[:, 0:1024]
                    acc = acc0 if ti < 8 else acc1
                    gsl = acc[:, ti % 8, ch * 1024:(ch + 1) * 1024]
                    v.wait_ge(s.sem("tc"), base["tc"] + n + 1)
                    v.wait_ge(s.sem("ab"), base["ab"] + ti + 1)
                    v.wait_ge(s.sem("sx" if par == 0 else "sx1"),
                              base["sx" if par == 0 else "sx1"]
                              + 16 * (n // 2 + 1))
                    if n >= 2:
                        ksem = "sy" if par == 0 else "sy1"
                        v.wait_ge(s.sem(ksem), base[ksem] + 16 * (n // 2))
                    v.scalar_tensor_tensor(outt[:, par, :], ps, 1.0, gsl,
                                           ALU.mult, ALU.mult)
                    v.drain()
                    v.scalar_tensor_tensor(outt2[:, par, :], outt[:, par, :],
                                           1.0, xfin[:, par, :],
                                           ALU.mult, ALU.add)
                    v.drain().then_inc(s.sem("vb"), 1)
                    s.bump("vb")
        block.vector(f_vec)

        def f_gp2(g):
            g.wait_ge(s.sem("sy"), s.cnt["sy"])
            g.wait_ge(s.sem("sy1"), s.cnt["sy1"])
        block.gpsimd(f_gp2)

    def build(self, upto="final"):
        nc = self.nc
        order = ["prepare", "A", "B", "thresh", "attn", "ffn1", "ffn2", "final"]
        fns = dict(prepare=self.phase_prepare, A=self.phase_A, B=self.phase_B,
                   thresh=self.phase_thresh, attn=self.phase_attn,
                   ffn1=self.phase_ffn1, ffn2=self.phase_ffn2,
                   final=self.phase_final)
        with nc.Block() as block:
            for name in order:
                fns[name](block)
                if name == upto:
                    break
            if upto != "final":
                self.phase_epilogue(block)
        return nc

    def phase_epilogue(self, block):
        s = self

        def f_gp(g):
            g.wait_ge(s.sem("cc"), s.cnt["cc"])
            g.wait_ge(s.sem("so"), s.cnt["so"])
            g.wait_ge(s.sem("sk"), s.cnt["sk"])
            g.wait_ge(s.sem("ve"), s.cnt["ve"])
            if s.debug:
                g.dma_start(out=s.log_dbg[:, :],
                            in_=s.log[:].rearrange("p a b -> p (a b)")
                            ).then_inc(s.sem("so"), 16)
                g.dma_start(out=s.rw_dbg[:, :],
                            in_=s.rw[:].rearrange("p a b -> p (a b)")
                            ).then_inc(s.sem("so"), 16)
                s.bump("so", 32)
            g.dma_start(out=s.out[:, :], in_=s.x_own[:, :]).then_inc(s.sem("so"), 16)
            s.bump("so", 16)
            g.wait_ge(s.sem("so"), s.cnt["so"])
        block.gpsimd(f_gp)


def _stage_inputs(inputs):
    x = np.asarray(inputs["L0_attn_x"])
    T = lambda a: np.ascontiguousarray(np.asarray(a).T)
    w = {
        "w_dkvT": T(inputs["L0_attn_W_DKV"]),
        "w_dqT": T(inputs["L0_attn_W_DQ"]),
        "w_uqT": T(inputs["L0_attn_W_UQ"]),
        "w_ukT": np.ascontiguousarray(np.asarray(inputs["L0_attn_W_UK"]).T),
        "w_uvT": np.ascontiguousarray(np.asarray(inputs["L0_attn_W_UV"]).T),
        "w_oT": T(inputs["L0_attn_W_O"]),
        "b_o": np.asarray(inputs["L0_attn_b_O"]).reshape(1, D),
        "gate_wT": T(inputs["L0_ffn_gate_w"]),
        "uT": T(inputs["L0_ffn_u"]),
        "gT": T(inputs["L0_ffn_g"]),
        "dT": T(inputs["L0_ffn_d"]),
    }
    tril = np.triu(np.ones((128, 128), dtype=np.float32))
    ident = np.eye(128, dtype=np.float32)
    in_maps = []
    for core in range(8):
        b, j = divmod(core, 4)
        xs = np.ascontiguousarray(x[b])
        c1 = x[b, j * 1024:(j + 1) * 1024]
        c2 = x[b, (7 - j) * 1024:(8 - j) * 1024]
        xo = np.ascontiguousarray(np.concatenate([c1, c2], axis=0))
        bias = np.zeros((128, 32), dtype=np.float32)
        for it in range(6):
            if it >= 2 * j:
                bias[:, it] = NEGB
        for it in range(14):
            if it >= 2 * (7 - j):
                bias[:, 8 + it] = NEGB
        m = dict(w)
        m.update({
            "x_seq": xs, "x_own": xo,
            "tril_in": tril, "bias_in": bias, "ident_in": ident,
        })
        in_maps.append(m)
    return in_maps


def kernel(**inputs):
    in_maps = _stage_inputs(inputs)
    mk = MK(debug=False)
    nc = mk.build()
    res = run_bass_kernel_spmd(nc, in_maps, core_ids=list(range(8)))
    x = np.asarray(inputs["L0_attn_x"])
    out = np.empty((B, SEQ, D), dtype=np.float32)
    for core in range(8):
        b, j = divmod(core, 4)
        o = res.results[core]["out"]
        out[b, j * 1024:(j + 1) * 1024] = o[0:1024]
        out[b, (7 - j) * 1024:(8 - j) * 1024] = o[1024:2048]
    return out
